# version 06
# baseline (speedup 1.0000x reference)
import math
from contextlib import ExitStack

import numpy as np
import concourse.bass as bass
import concourse.mybir as mybir
from concourse.bass_utils import run_bass_kernel_spmd

F32 = mybir.dt.float32
BF16 = mybir.dt.bfloat16
I32 = mybir.dt.int32
AF = mybir.ActivationFunctionType
ALU = mybir.AluOpType
AX = mybir.AxisListType

T = 4096
DM = 4096
H = 32
DH = 128
NI = 32
NOWN = 1024
NSTRIP = T // 128
NOSTRIP = NOWN // 128
EPS = 1e-5
ASCALE = 1.0 / math.sqrt(DH)
K0 = -0.5 * math.log(DH)
GS = 64
NG = DM // GS
RANK = 16
LORA_SC = 1.0 / RANK

ENGS = ("sync", "gpsimd", "tensor", "vector", "scalar")
SEMS = ("dPS", "dPG", "dX0", "dX1", "dWQ0", "dWQ1", "dWO0", "dWO1",
        "dO0", "dO1", "cT", "cV", "cS")


class Prog:
    def __init__(self):
        self.ops = {e: [] for e in ENGS}
        self.cnt = {s: 0 for s in SEMS}

    def op(self, eng, fn, wait=(), inc=None, n=1, tag=""):
        waits = [w for w in wait if w is not None]
        evt = None
        if inc is not None:
            self.cnt[inc] += n
            evt = (inc, self.cnt[inc])
        self.ops[eng].append((fn, waits, (inc, n) if inc else None, tag))
        return evt

    def dma(self, eng, out_ap, in_ap, sem, wait=(), tag=""):
        return self.op(eng, lambda e, o=out_ap, i=in_ap: e.dma_start(out=o, in_=i),
                       wait=wait, inc=sem, n=16, tag=tag)


LAST_PROG = None


def build_nc(stop_after=None):
    global LAST_PROG
    nc = bass.Bass("TRN2", target_bir_lowering=False)

    d_x = nc.dram_tensor("x_b", [T, DM], F32, kind="ExternalInput").ap()
    d_xo = nc.dram_tensor("x_own", [NOWN, DM], F32, kind="ExternalInput").ap()
    d_wq = nc.dram_tensor("wq", [DM, DM], F32, kind="ExternalInput").ap()
    d_wk = nc.dram_tensor("wk", [DM, DH], F32, kind="ExternalInput").ap()
    d_wv = nc.dram_tensor("wv", [DM, DH], F32, kind="ExternalInput").ap()
    d_wo = nc.dram_tensor("wo", [DM, DM], F32, kind="ExternalInput").ap()
    d_qw = nc.dram_tensor("qweight", [DM, DM], I32, kind="ExternalInput").ap()
    d_sc = nc.dram_tensor("scales", [DM, NG], F32, kind="ExternalInput").ap()
    d_zr = nc.dram_tensor("zeros", [DM, NG], F32, kind="ExternalInput").ap()
    d_lA = nc.dram_tensor("lora_A", [RANK, DM], F32, kind="ExternalInput").ap()
    d_lB = nc.dram_tensor("lora_B", [DM, RANK], F32, kind="ExternalInput").ap()
    d_bias = nc.dram_tensor("bias", [DM], F32, kind="ExternalInput").ap()
    d_idb = nc.dram_tensor("ident_bf", [128, 128], BF16, kind="ExternalInput").ap()
    d_idf = nc.dram_tensor("ident_f", [128, 128], F32, kind="ExternalInput").ap()
    d_msk = nc.dram_tensor("masks01", [4, 128, 128], BF16, kind="ExternalInput").ap()
    d_out = nc.dram_tensor("out_own", [NOWN, DM], F32, kind="ExternalOutput").ap()

    prog = Prog()
    LAST_PROG = prog
    es = ExitStack()
    ev = {}
    open_guards = []

    def sb(name, shape, dtype, side="left"):
        return es.enter_context(nc.sbuf_tensor(name, shape, dtype, side=side))

    ones_col = sb("ones_col", [128, 1], BF16)
    ones_row = sb("ones_row", [1, 128], BF16)
    eps_t = sb("eps_t", [128, 1], F32)
    k0_t = sb("k0_t", [1, 1], F32)
    ident_bf = sb("ident_bf_t", [128, 128], BF16)
    ident_f = sb("ident_f_t", [128, 128], F32)
    sums_t = sb("sums_t", [128, 2], F32)
    sumsq_t = sb("sumsq_t", [128, 2], F32)
    mean_t = sb("mean_t", [128, 2], F32)
    m2_t = sb("m2_t", [128, 2], F32)
    m2e_t = sb("m2e_t", [128, 2], F32)
    rstd_t = sb("rstd_t", [128, 2], F32)
    qm_all = sb("qm_all", [1, 32], F32)
    kmx_t = sb("kmx_t", [1, 8], F32)
    km2_t = sb("km2_t", [1, 1], F32)
    t32_t = sb("t32_t", [1, 32], F32)
    w32_t = sb("w32_t", [1, 32], F32)
    brow_t = sb("brow_t", [1, 32], BF16)
    bias_c = sb("bias_c", [128, 32], F32)
    A_t = sb("A_t", [128, NI * RANK], BF16)
    B_t = sb("B_t", [16, DM], BF16)
    u_sc = sb("u_sc", [16, NOWN], BF16)
    bias_t = sb("bias_t", [128, 32], F32)
    masks = sb("masks_t", [128, 4, 128], BF16)
    scale_col = sb("scale_col", [128, 1], F32)
    mean_own = sb("mean_own", [128, NOSTRIP], F32)
    rstd_own = sb("rstd_own", [128, NOSTRIP], F32)

    prog.dma("sync", ident_bf[:], d_idb, "dPS")
    prog.dma("sync", ident_f[:], d_idf, "dPS")
    for j in range(4):
        e_consts = prog.dma("sync", masks[:, j, :], d_msk[j], "dPS")
    ev["consts"] = e_consts
    e_ones = prog.op("vector", lambda e: e.memset(ones_col[:], 1.0), inc="cV")
    prog.op("vector", lambda e: e.memset(ones_row[:], 1.0), inc="cV")
    prog.op("vector", lambda e: e.memset(eps_t[:], EPS), inc="cV")
    e_k0 = prog.op("vector", lambda e: e.memset(k0_t[:], K0), inc="cV")
    e_c128 = prog.op("vector", lambda e: e.memset(scale_col[:], 128.0), inc="cV")
    e_sl = prog.op("scalar", lambda e: e.activation(scale_col[:], scale_col[:], AF.Ln,
                                                    bias=0.0, scale=1.0),
                   wait=[e_c128], inc="cS")
    e_scol = prog.op("scalar", lambda e: e.activation(scale_col[:], scale_col[:],
                                                      AF.Exp, bias=0.0, scale=-0.5),
                     wait=[e_sl], inc="cS")
    ev["scale_col"] = e_scol


    def bisect_finish(tile_ap, wait_ev):
        e_bo = prog.dma("sync", d_out[0:128, :], tile_ap, "dO0", wait=[wait_ev])
        prog.op("sync", None, wait=[e_bo])

    def make_ln(xstrip, xn_nat, ptrs, key):
        def ln_strip(src_dram, row0, s, dst3, col0, prev, save_idx=None):
            eA_prev, eR_prev, eT_prev = prev
            p = s % 2
            e_ld = prog.dma("sync", xstrip[:, p, :], src_dram[row0:row0 + 128, :],
                            "dX0" if p == 0 else "dX1",
                            wait=[eA_prev], tag=f"{key}xld{s}")
            e_sum = prog.op("vector", lambda e, p=p: e.reduce_sum(
                sums_t[:, p:p + 1], xstrip[:, p, :], axis=AX.X),
                wait=[e_ld, eR_prev], inc="cV")
            e_sq = prog.op("scalar", lambda e, p=p: e.activation(
                xn_nat[:, p, :], xstrip[:, p, :], AF.Square,
                accum_out=sumsq_t[:, p:p + 1]), wait=[e_ld, eT_prev], inc="cS")
            e_mean = prog.op("scalar", lambda e, p=p: e.mul(
                mean_t[:, p:p + 1], sums_t[:, p:p + 1], 1.0 / DM),
                wait=[e_sum, eA_prev], inc="cS")
            e_m2 = prog.op("vector", lambda e, p=p: e.tensor_mul(
                m2_t[:, p:p + 1], mean_t[:, p:p + 1], mean_t[:, p:p + 1]),
                wait=[e_mean], inc="cV")
            e_m2e = prog.op("vector", lambda e, p=p: e.tensor_scalar(
                m2e_t[:, p:p + 1], m2_t[:, p:p + 1], -1.0, EPS, ALU.mult, ALU.add),
                wait=[e_m2], inc="cV")
            e_ln = prog.op("scalar", lambda e, p=p: e.activation(
                rstd_t[:, p:p + 1], sumsq_t[:, p:p + 1], AF.Ln,
                bias=m2e_t[:, p:p + 1], scale=1.0 / DM),
                wait=[e_m2e, e_sq], inc="cS")
            e_rstd = prog.op("scalar", lambda e, p=p: e.activation(
                rstd_t[:, p:p + 1], rstd_t[:, p:p + 1], AF.Exp, bias=0.0, scale=-0.5),
                wait=[e_ln], inc="cS")
            e_apply = prog.op("vector", lambda e, p=p: e.tensor_scalar(
                xn_nat[:, p, :], xstrip[:, p, :], mean_t[:, p:p + 1],
                rstd_t[:, p:p + 1], ALU.subtract, ALU.mult),
                wait=[e_rstd], inc="cV")
            if save_idx is not None:
                prog.op("scalar", lambda e, p=p, si=save_idx: e.copy(
                    mean_own[:, si:si + 1], mean_t[:, p:p + 1]), wait=[e_rstd])
                prog.op("scalar", lambda e, p=p, si=save_idx: e.copy(
                    rstd_own[:, si:si + 1], rstd_t[:, p:p + 1]))
            e_evac = None
            tr_last = None
            for j in range(8):
                pj = ptrs[j % 2]
                for u in range(4):
                    ic = 4 * j + u
                    w = []
                    if u == 0:
                        w = [e_apply, ev.get((key, "ptr_free", j % 2))]
                    tr_last = prog.op("tensor", lambda e, p=p, ic=ic, u=u, pj=pj:
                                      e.matmul(pj[:, 128 * u:128 * (u + 1)],
                                               xn_nat[:, p, 128 * ic:128 * (ic + 1)],
                                               ident_bf[:], start=True, stop=True),
                                      wait=w, inc="cT" if u == 3 else None)
                e_evac = prog.op("scalar", lambda e, j=j, pj=pj, dst3=dst3, c0=col0:
                                 e.copy(dst3[:, 4 * j:4 * (j + 1), c0:c0 + 128],
                                        pj[:].rearrange("p (a b) -> p a b", b=128)),
                                 wait=[tr_last], inc="cS")
                ev[(key, "ptr_free", j % 2)] = e_evac
            return (e_apply, e_rstd, tr_last), e_evac
        return ln_strip

    g_K = nc.sbuf_tensor("K_fm", [128, T], BF16)
    K_fm = g_K.__enter__()
    open_guards.append(g_K)
    g_V = nc.sbuf_tensor("V_tok", [128, NI, 128], BF16)
    V_tok = g_V.__enter__()
    open_guards.append(g_V)

    g_xn = nc.sbuf_tensor("xn_own", [128, NI, NOWN], BF16, side="right")
    xn_own = g_xn.__enter__()
    open_guards.append(g_xn)

    with nc.sbuf_tensor("wk_bf", [128, NI, DH], BF16) as wk_bf, \
         nc.sbuf_tensor("wv_bf", [128, NI, DH], BF16) as wv_bf, \
         nc.sbuf_tensor("xstrip", [128, 2, DM], F32) as xstrip, \
         nc.sbuf_tensor("xn_nat", [128, 2, DM], BF16) as xn_nat, \
         nc.sbuf_tensor("cbuf", [128, NI, 512], BF16) as cbuf, \
         nc.sbuf_tensor("vfm_scr", [128, 512], BF16) as vfm_scr, \
         nc.psum_tensor("ptr_a", [128, 512], F32) as ptr_a, \
         nc.psum_tensor("ptr_b", [128, 512], F32) as ptr_b, \
         nc.psum_tensor("pK", [128, 512], F32) as pK, \
         nc.psum_tensor("pV", [128, 512], F32) as pV:

        ev[("A", "ptr_free", 0)] = ev["consts"]
        ev[("A", "ptr_free", 1)] = ev["consts"]
        ln_strip = make_ln(xstrip, xn_nat, [ptr_a, ptr_b], "A")
        prog.dma("gpsimd", wk_bf[:], d_wk.rearrange("(c p) d -> p c d", p=128), "dPG")
        e_wv = prog.dma("gpsimd", wv_bf[:], d_wv.rearrange("(c p) d -> p c d", p=128),
                        "dPG")
        e_wk = e_wv

        prev = (None, None, None)
        e_Kev = None
        for C in range(8):
            last_evac = None
            for r in range(4):
                s = 4 * C + r
                prev, evc = ln_strip(d_x, 128 * s, s, cbuf, 128 * r, prev)
                last_evac = evc
            mmK = None
            for ic in range(NI):
                w = [last_evac, e_wk, e_Kev] if ic == 0 else []
                mmK = prog.op("tensor", lambda e, ic=ic: e.matmul(
                    pK[:], wk_bf[:, ic, :], cbuf[:, ic, :],
                    start=(ic == 0), stop=(ic == NI - 1)),
                    wait=w, inc="cT" if ic == NI - 1 else None)
            mmV = None
            for ic in range(NI):
                w = [e_wv] if ic == 0 else []
                mmV = prog.op("tensor", lambda e, ic=ic: e.matmul(
                    pV[:], wv_bf[:, ic, :], cbuf[:, ic, :],
                    start=(ic == 0), stop=(ic == NI - 1)),
                    wait=w, inc="cT" if ic == NI - 1 else None)
            e_Kev = prog.op("scalar", lambda e, C=C: e.copy(
                K_fm[:, 512 * C:512 * (C + 1)], pK[:]), wait=[mmK], inc="cS")
            e_Vev = prog.op("scalar", lambda e: e.copy(vfm_scr[:], pV[:]),
                            wait=[mmV], inc="cS")
            tr = None
            for r2 in range(4):
                w = [e_Vev]
                if r2 == 0:
                    w.append(ev.get(("A", "ptr_free", 0)))
                tr = prog.op("tensor", lambda e, r2=r2: e.matmul(
                    ptr_a[:, 128 * r2:128 * (r2 + 1)],
                    vfm_scr[:, 128 * r2:128 * (r2 + 1)], ident_bf[:],
                    start=True, stop=True),
                    wait=w, inc="cT" if r2 == 3 else None)
            e_e2 = prog.op("scalar", lambda e, C=C: e.copy(
                V_tok[:, 4 * C:4 * C + 4, :],
                ptr_a[:].rearrange("p (a b) -> p a b", b=128)),
                wait=[tr], inc="cS")
            ev[("A", "ptr_free", 0)] = e_e2

        e_xn = None
        for s in range(NOSTRIP):
            prev, evc = ln_strip(d_xo, 128 * s, NSTRIP + s, xn_own, 128 * s, prev,
                                 save_idx=s)
            e_xn = evc
        ev["xn_done"] = e_xn
        ev["a_prev"] = prev
        if stop_after == "A":
            bisect_finish(xstrip[:, 0, :], ev["xn_done"])
            ev["early"] = True

    if stop_after == "A":
        return _finish(nc, es, prog, open_guards)
    g_q = nc.sbuf_tensor("q_own", [128, H, NOWN], BF16)
    q_own = g_q.__enter__()
    open_guards.append(g_q)

    with nc.sbuf_tensor("wq_bf", [128, 2, NI, 128], BF16) as wq_bf, \
         nc.sbuf_tensor("qsq_scr", [128, NOWN], BF16) as qsq_scr, \
         nc.sbuf_tensor("ksq_scr", [128, T], BF16) as ksq_scr, \
         nc.sbuf_tensor("lA_bf", [16, DM], BF16) as lA_bf, \
         nc.psum_tensor("pq_a", [128, 1024], F32) as pq_a, \
         nc.psum_tensor("pq_b", [128, 1024], F32) as pq_b, \
         nc.psum_tensor("pqsq", [128, 1024], F32) as pqsq, \
         nc.psum_tensor("pAt", [128, 512], F32) as pAt:

        pq = [pq_a, pq_b]
        wq_view = d_wq.rearrange("(c p) o -> p c o", p=128)
        e_wq = {}
        for d0 in range(2):
            e_wq[d0] = prog.dma("gpsimd", wq_bf[:, d0, :, :],
                                wq_view[:, :, 128 * d0:128 * (d0 + 1)],
                                "dWQ0" if d0 % 2 == 0 else "dWQ1",
                                wait=[ev["xn_done"]], tag=f"wq{d0}")
        e_lA = prog.dma("gpsimd", lA_bf[:], d_lA, "dPG", wait=[ev["xn_done"]])

        e_qev = {}
        e_red = {}
        e_qsqmm = {}
        for dout in range(H):
            par = dout % 2
            mm_last = None
            for tc in range(2):
                for ic in range(NI):
                    w = []
                    if tc == 0 and ic == 0:
                        w = [e_wq[dout], ev["xn_done"], e_qev.get(dout - 2)]
                    mm_last = prog.op("tensor", lambda e, par=par, tc=tc, ic=ic:
                                      e.matmul(pq[par][:, 512 * tc:512 * (tc + 1)],
                                               wq_bf[:, par, ic, :],
                                               xn_own[:, ic, 512 * tc:512 * (tc + 1)],
                                               start=(ic == 0), stop=(ic == NI - 1)),
                                      wait=w,
                                      inc="cT" if (tc == 1 and ic == NI - 1) else None)
            e_qev[dout] = prog.op("scalar", lambda e, par=par, dout=dout: e.copy(
                q_own[:, dout, :], pq[par][:]), wait=[mm_last], inc="cS")
            if dout + 2 < H:
                e_wq[dout + 2] = prog.dma(
                    "gpsimd", wq_bf[:, par, :, :],
                    wq_view[:, :, 128 * (dout + 2):128 * (dout + 3)],
                    "dWQ0" if par == 0 else "dWQ1",
                    wait=[e_qev[dout]], tag=f"wq{dout + 2}")
            e_sqv = prog.op("vector", lambda e, dout=dout: e.tensor_mul(
                qsq_scr[:], q_own[:, dout, :], q_own[:, dout, :]),
                wait=[e_qev[dout], e_qsqmm.get(dout - 1)], inc="cV")
            mm2 = None
            for tc in range(2):
                w = [e_sqv, e_ones, e_red.get(dout - 1)] if tc == 0 else []
                mm2 = prog.op("tensor", lambda e, tc=tc: e.matmul(
                    pqsq[0:1, 512 * tc:512 * (tc + 1)], ones_col[:],
                    qsq_scr[:, 512 * tc:512 * (tc + 1)], start=True, stop=True),
                    wait=w, inc="cT" if tc == 1 else None)
            e_qsqmm[dout] = mm2
            e_red[dout] = prog.op("vector", lambda e, dout=dout: e.reduce_max(
                qm_all[0:1, dout:dout + 1], pqsq[0:1, 0:1024], axis=AX.X),
                wait=[mm2], inc="cV")

        e_ks = prog.op("vector", lambda e: e.tensor_mul(ksq_scr[:], K_fm[:], K_fm[:]),
                       inc="cV")
        e_prev_red = e_red[H - 1]
        for C in range(8):
            mmk = prog.op("tensor", lambda e, C=C: e.matmul(
                pqsq[0:1, 0:512], ones_col[:], ksq_scr[:, 512 * C:512 * (C + 1)],
                start=True, stop=True), wait=[e_ks, e_prev_red], inc="cT")
            e_prev_red = prog.op("vector", lambda e, C=C: e.reduce_max(
                kmx_t[0:1, C:C + 1], pqsq[0:1, 0:512], axis=AX.X),
                wait=[mmk], inc="cV")
        e_sp = prog.op("scalar", lambda e: e.copy(km2_t[:], kmx_t[0:1, 0:1]),
                       wait=[e_prev_red], inc="cS")
        e_km2 = prog.op("vector", lambda e: e.reduce_max(km2_t[:], kmx_t[:], axis=AX.X),
                        wait=[e_sp, e_prev_red], inc="cV")
        e_c1 = prog.op("scalar", lambda e: e.mul(t32_t[:], qm_all[:], km2_t[0:1, 0:1]),
                       wait=[e_km2, e_k0], inc="cS")
        e_c2 = prog.op("scalar", lambda e: e.activation(t32_t[:], t32_t[:], AF.Ln,
                                                        bias=0.0, scale=1.0),
                       wait=[e_c1], inc="cS")
        e_c3 = prog.op("scalar", lambda e: e.activation(w32_t[:], t32_t[:], AF.Exp,
                                                        bias=k0_t[0:1, 0:1], scale=0.5),
                       wait=[e_c2], inc="cS")
        e_brow = prog.op("scalar", lambda e: e.mul(brow_t[:], w32_t[:], -1.0),
                         wait=[e_c3], inc="cS")
        e_bcm = prog.op("tensor", lambda e: e.matmul(
            pqsq[:, 0:32], ones_row[:], brow_t[:], start=True, stop=True),
            wait=[e_brow], inc="cT")
        ev["bias_c"] = prog.op("scalar", lambda e: e.copy(bias_c[:], pqsq[:, 0:32]),
                               wait=[e_bcm], inc="cS")

        A3 = A_t[:].rearrange("p (a b) -> p a b", b=RANK)
        tr = None
        for ic in range(NI):
            tr = prog.op("tensor", lambda e, ic=ic: e.matmul(
                pAt[:, RANK * ic:RANK * (ic + 1)], lA_bf[:, 128 * ic:128 * (ic + 1)],
                ident_bf[0:16, 0:16], start=True, stop=True),
                wait=[e_lA] if ic == 0 else [],
                inc="cT" if ic == NI - 1 else None)
        e_Atev = prog.op("scalar", lambda e: e.copy(A_t[:, 0:NI * RANK], pAt[:, 0:NI * RANK]),
                         wait=[tr], inc="cS")
        e_udone = None
        for tc in range(2):
            mmu = None
            for ic in range(NI):
                w = [e_Atev] if ic == 0 and tc == 0 else []
                mmu = prog.op("tensor", lambda e, tc=tc, ic=ic, A3=A3: e.matmul(
                    pq_a[0:16, 512 * tc:512 * (tc + 1)], A3[:, ic, :],
                    xn_own[:, ic, 512 * tc:512 * (tc + 1)],
                    start=(ic == 0), stop=(ic == NI - 1)),
                    wait=w, inc="cT" if ic == NI - 1 else None)
            e_udone = prog.op("vector", lambda e, tc=tc: e.tensor_single_scalar(
                u_sc[:, 512 * tc:512 * (tc + 1)], pq_a[0:16, 512 * tc:512 * (tc + 1)],
                LORA_SC, ALU.mult), wait=[mmu], inc="cV")
        ev["u_done"] = e_udone
        if stop_after == "B":
            e_bo = prog.dma("sync", d_out[0:128, 0:2048],
                            ksq_scr[:, 0:4096].bitcast(F32), "dO0",
                            wait=[ev["u_done"]])
            prog.op("sync", None, wait=[e_bo])

    if stop_after == "B":
        return _finish(nc, es, prog, open_guards)
    g_xn.__exit__(None, None, None)
    open_guards.remove(g_xn)
    g_attn = nc.sbuf_tensor("attn_own", [128, NI, NOWN], BF16, side="right")
    attn_own = g_attn.__enter__()
    open_guards.append(g_attn)

    with nc.sbuf_tensor("probsT", [128, 3, 1024], BF16) as probsT, \
         nc.sbuf_tensor("zeros_bf", [128, 512], BF16) as zeros_bf, \
         nc.sbuf_tensor("recip_sb", [1, 512], F32) as recip_sb, \
         nc.sbuf_tensor("recip_bf", [1, 512], BF16) as recip_bf, \
         nc.sbuf_tensor("recip_bc", [128, 512], F32) as recip_bc, \
         nc.psum_tensor("pS_a", [128, 1024], F32) as pS_a, \
         nc.psum_tensor("pS_b", [128, 1024], F32) as pS_b, \
         nc.psum_tensor("pAV", [128, 512], F32) as pAV, \
         nc.psum_tensor("psums", [128, 512], F32) as psums, \
         nc.psum_tensor("pbc", [128, 512], F32) as pbc:

        pS = [pS_a, pS_b]
        e_ms0 = prog.op("vector", lambda e: e.memset(pS_a[:], 0.0),
                        wait=[ev["u_done"]], inc="cV")
        e_ms1 = prog.op("vector", lambda e: e.memset(pS_b[:], 0.0),
                        wait=[ev["u_done"]], inc="cV")
        e_zero = prog.op("vector", lambda e: e.memset(zeros_bf[:], 0.0),
                         wait=[ev["u_done"]], inc="cV")
        eS_free = {0: e_ms0, 1: e_ms1}
        eP_free = {0: None, 1: None, 2: None}
        e_avread = None
        e_sumread = None
        pslot_i = 0
        for h in range(H):
            for beta in range(2):
                nkc = 16 * (beta + 1)
                e_pair_last = None
                for kc0 in range(0, nkc, 2):
                    slot = (kc0 // 2) % 2
                    w_kc = 512 - 128 * max(0, kc0 // 4 - 4 * beta)
                    off = 512 * beta + (512 - w_kc)
                    mmS = None
                    for kk in range(2):
                        w = []
                        if kk == 0:
                            w = [eS_free[slot]]
                            if h == 0 and beta == 0 and kc0 == 0:
                                w.append(ev["bias_c"])
                        kc = kc0 + kk
                        mmS = prog.op("tensor", lambda e, slot=slot, kk=kk, kc=kc,
                                      w_kc=w_kc, h=h, off=off: e.matmul(
                            pS[slot][:, 512 * kk:512 * kk + w_kc],
                            K_fm[:, 128 * kc:128 * (kc + 1)],
                            q_own[:, h, off:off + w_kc],
                            start=True, stop=True),
                            wait=w, inc="cT" if kk == 1 else None)
                    ps_i = pslot_i % 3
                    pslot_i += 1
                    e_exp = prog.op("scalar", lambda e, slot=slot, ps_i=ps_i,
                                    w_kc=w_kc, h=h: e.activation(
                        probsT[:, ps_i, 0:512 + w_kc], pS[slot][:, 0:512 + w_kc],
                        AF.Exp, bias=bias_c[:, h:h + 1], scale=scale_col[:]),
                        wait=[mmS, eP_free[ps_i],
                              ev["scale_col"] if (h == 0 and beta == 0) else None],
                        inc="cS")
                    eS_free[slot] = e_exp
                    e_gate = e_exp
                    if kc0 // 4 >= 4 * beta:
                        for kk in range(2):
                            j = (kc0 + kk) % 4
                            e_gate = prog.op("vector", lambda e, ps_i=ps_i, kk=kk,
                                             w_kc=w_kc, j=j: e.tensor_mul(
                                probsT[:, ps_i, 512 * kk:512 * kk + 128],
                                probsT[:, ps_i, 512 * kk:512 * kk + 128],
                                masks[:, j, :]),
                                wait=[e_exp, ev["consts"]] if kk == 0 else [],
                                inc="cV" if kk == 1 else None)
                    last = None
                    for kk in range(2):
                        kc = kc0 + kk
                        qoff = 512 - w_kc
                        w = []
                        if kk == 0:
                            w = [e_gate]
                            if kc == 0:
                                w += [e_avread, e_sumread]
                        prog.op("tensor", lambda e, kc=kc, ps_i=ps_i, kk=kk,
                                w_kc=w_kc, qoff=qoff: e.matmul(
                            pAV[:, qoff:512], V_tok[:, kc, :],
                            probsT[:, ps_i, 512 * kk:512 * kk + w_kc],
                            start=(kc == 0), stop=False), wait=w)
                        last = prog.op("tensor", lambda e, kc=kc, ps_i=ps_i, kk=kk,
                                       w_kc=w_kc, qoff=qoff: e.matmul(
                            psums[0:1, qoff:512], ones_col[:],
                            probsT[:, ps_i, 512 * kk:512 * kk + w_kc],
                            start=(kc == 0), stop=False), inc="cT")
                    eP_free[ps_i] = last
                    e_pair_last = last
                prog.op("tensor", lambda e: e.matmul(
                    pAV[:], V_tok[:, 0, :], zeros_bf[:], start=False, stop=True),
                    wait=[e_zero])
                e_pair_last = prog.op("tensor", lambda e: e.matmul(
                    psums[0:1, :], ones_col[:], zeros_bf[:], start=False, stop=True),
                    inc="cT")
                e_rc = prog.op("vector", lambda e: e.reciprocal(
                    recip_sb[:], psums[0:1, :]), wait=[e_pair_last], inc="cV")
                e_sumread = e_rc
                e_rcp = prog.op("scalar", lambda e: e.copy(recip_bf[:], recip_sb[:]),
                                wait=[e_rc], inc="cS")
                e_bcm2 = prog.op("tensor", lambda e: e.matmul(
                    pbc[:], ones_row[:], recip_bf[:], start=True, stop=True),
                    wait=[e_rcp], inc="cT")
                e_bce = prog.op("scalar", lambda e: e.copy(recip_bc[:], pbc[:]),
                                wait=[e_bcm2], inc="cS")
                e_avread = prog.op("vector", lambda e, h=h, beta=beta: e.tensor_mul(
                    attn_own[:, h, 512 * beta:512 * (beta + 1)], pAV[:], recip_bc[:]),
                    wait=[e_bce], inc="cV")
        ev["attn_done"] = e_avread
        if stop_after == "C":
            e_bo = prog.dma("sync", d_out[0:128, 0:512], recip_bc[:, 0:512],
                            "dO0", wait=[ev["attn_done"]])
            prog.op("sync", None, wait=[e_bo])

    if stop_after == "C":
        return _finish(nc, es, prog, open_guards)

    for g in (g_q, g_V, g_K):
        g.__exit__(None, None, None)
        open_guards.remove(g)

    g_xn2 = nc.sbuf_tensor("xn_own2", [128, NI, NOWN], BF16, side="right")
    xn2 = g_xn2.__enter__()
    open_guards.append(g_xn2)

    with nc.sbuf_tensor("xstrip2", [128, 2, DM], F32) as xstrip2, \
         nc.sbuf_tensor("xn_nat2", [128, 2, DM], BF16) as xn_nat2, \
         nc.sbuf_tensor("b32_scr", [32, 128], BF16) as b32_scr, \
         nc.sbuf_tensor("lB_bf", [128, NI, RANK], BF16) as lB_bf, \
         nc.psum_tensor("ptr2_a", [128, 512], F32) as ptr2_a, \
         nc.psum_tensor("ptr2_b", [128, 512], F32) as ptr2_b, \
         nc.psum_tensor("pmisc", [128, 512], F32) as pmisc, \
         nc.psum_tensor("pmiscb", [128, 512], F32) as pmiscb:

        ptr2 = [ptr2_a, ptr2_b]
        eA_prev = ev["attn_done"]
        eT_prev2 = ev["a_prev"][2]
        evc = None
        nreb = 0 if stop_after == "D1b" else (2 if stop_after in ("D1c", "D1d") else NOSTRIP)
        skip_tr = stop_after == "D1d"
        for s in range(nreb):
            p = s % 2
            e_ld = prog.dma("sync", xstrip2[:, p, :], d_xo[128 * s:128 * (s + 1), :],
                            "dX0" if p == 0 else "dX1", wait=[eA_prev],
                            tag=f"Dxld{s}")
            e_apply = prog.op("vector", lambda e, p=p, s=s: e.tensor_scalar(
                xn_nat2[:, p, :], xstrip2[:, p, :], mean_own[:, s:s + 1],
                rstd_own[:, s:s + 1], ALU.subtract, ALU.mult),
                wait=[e_ld, eT_prev2], inc="cV")
            eA_prev = e_apply
            if skip_tr:
                evc = e_apply
                continue
            tr_last = None
            for j in range(8):
                pj = ptr2[j % 2]
                for u in range(4):
                    ic = 4 * j + u
                    w = []
                    if u == 0:
                        w = [e_apply, ev.get(("D", "ptr_free", j % 2))]
                    tr_last = prog.op("tensor", lambda e, p=p, ic=ic, u=u, pj=pj:
                                      e.matmul(pj[:, 128 * u:128 * (u + 1)],
                                               xn_nat2[:, p, 128 * ic:128 * (ic + 1)],
                                               ident_bf[:], start=True, stop=True),
                                      wait=w, inc="cT" if u == 3 else None)
                evc = prog.op("scalar", lambda e, j=j, pj=pj, s=s: e.copy(
                    xn2[:, 4 * j:4 * (j + 1), 128 * s:128 * s + 128],
                    pj[:].rearrange("p (a b) -> p a b", b=128)),
                    wait=[tr_last], inc="cS")
                ev[("D", "ptr_free", j % 2)] = evc
            eT_prev2 = tr_last
        ev["xn2_done"] = evc if evc is not None else ev["attn_done"]
        if stop_after in ("D1c", "D1d"):
            e_bo = prog.dma("sync", d_out[0:128, 0:1024],
                            xn_nat2[:, 0, :].bitcast(F32)[:, 0:1024], "dO0",
                            wait=[ev["xn2_done"]])
            prog.op("sync", None, wait=[e_bo])
        if stop_after == "D1a":
            e_bo = prog.dma("sync", d_out[0:128, 0:1024],
                            xn_nat2[:, 0, :].bitcast(F32)[:, 0:1024], "dO0",
                            wait=[ev["xn2_done"]])
            prog.op("sync", None, wait=[e_bo])
        if stop_after == "D1c":
            pass

        prog.dma("gpsimd", b32_scr[:],
                 d_bias.rearrange("(c p) -> c p", p=128), "dPG",
                 wait=[ev["attn_done"]])
        e_lB = prog.dma("gpsimd", lB_bf[:], d_lB.rearrange("(c p) r -> p c r", p=128),
                        "dPG")
        trb = prog.op("tensor", lambda e: e.matmul(
            pmisc[:, 0:32], b32_scr[:], ident_bf[0:32, 0:32],
            start=True, stop=True), wait=[e_lB], inc="cT")
        ev["bias_t"] = prog.op("scalar", lambda e: e.copy(bias_t[:], pmisc[:, 0:32]),
                               wait=[trb], inc="cS")
        e_Bt = {}
        for g in range(NI // 4):
            trc = None
            for u in range(4):
                cc = 4 * g + u
                w = []
                if u == 0:
                    w = [e_lB, e_Bt.get(g - 1)]
                trc = prog.op("tensor", lambda e, cc=cc, u=u: e.matmul(
                    pmiscb[0:16, 128 * u:128 * (u + 1)], lB_bf[:, cc, :],
                    ident_bf[:], start=True, stop=True),
                    wait=w, inc="cT" if u == 3 else None)
            e_Bt[g] = prog.op("scalar", lambda e, g=g: e.copy(
                B_t[:, 512 * g:512 * (g + 1)], pmiscb[0:16, 0:512]),
                wait=[trc], inc="cS")
        ev["B_t"] = e_Bt[NI // 4 - 1]
        if stop_after == "D1b":
            e_bo = prog.dma("sync", d_out[0:16, 0:512],
                            B_t[0:16, 0:1024].bitcast(F32),
                            "dO0", wait=[ev["B_t"], ev["bias_t"]])
            prog.op("sync", None, wait=[e_bo])
        if stop_after == "D1":
            e_bo = prog.dma("sync", d_out[0:128, 0:1024],
                            xn_nat2[:, 0, :].bitcast(F32)[:, 0:1024], "dO0",
                            wait=[ev["xn2_done"], ev["B_t"], ev["bias_t"]])
            prog.op("sync", None, wait=[e_bo])

    if stop_after in ("D1", "D1a", "D1b", "D1c", "D1d"):
        return _finish(nc, es, prog, open_guards)

    with nc.sbuf_tensor("qw_bf", [128, DM], BF16) as qw_bf, \
         nc.sbuf_tensor("qs_scr", [128, 1024], F32) as qs_scr, \
         nc.sbuf_tensor("wn_strip", [128, 2, 1024], BF16) as wn_strip, \
         nc.sbuf_tensor("Wt", [128, 2, NI, 128], BF16) as Wt, \
         nc.sbuf_tensor("wo_bf", [128, 2, NI, 128], BF16) as wo_bf, \
         nc.sbuf_tensor("sz_t", [128, 2, 2, GS], F32) as sz_t, \
         nc.sbuf_tensor("zs_t", [128, 2, GS], F32) as zs_t, \
         nc.sbuf_tensor("s_fm", [128, 1024], BF16) as s_fm, \
         nc.sbuf_tensor("ostage", [128, 8, 128], F32) as ostage, \
         nc.sbuf_tensor("xsl", [128, 8, 128], F32) as xsl, \
         nc.psum_tensor("pout_a", [128, 1024], F32) as pout_a, \
         nc.psum_tensor("pout_b", [128, 1024], F32) as pout_b, \
         nc.psum_tensor("pWt_a", [128, 512], F32) as pWt_a, \
         nc.psum_tensor("pWt_b", [128, 512], F32) as pWt_b, \
         nc.psum_tensor("pT_a", [128, 512], F32) as pT_a, \
         nc.psum_tensor("pT_b", [128, 512], F32) as pT_b:

        pout = [pout_a, pout_b]
        pWt = [pWt_a, pWt_b]
        pT = [pT_a, pT_b]
        wo_view = d_wo.rearrange("(c p) o -> p c o", p=128)
        e_qwld = {}
        e_szld = {}
        e_wold = {}
        e_Wt_done = {}
        e_mm_done = {}
        e_sfm = {}
        e_trlast = {}

        def prefetch(oc, qw_too=True):
            if oc >= NI:
                return
            par = oc % 2
            gate = ([ev["attn_done"], ev["xn2_done"], ev["B_t"], ev["bias_t"]]
                    if oc < 2 else [])
            sq = "dWQ0" if par == 0 else "dWQ1"
            if qw_too:
                prog.dma("gpsimd", qw_bf[:], d_qw[128 * oc:128 * (oc + 1), :],
                         sq, wait=gate + [e_Wt_done.get(oc - 1)], tag=f"qw{oc}")
            prog.dma("gpsimd", sz_t[:, par, 0, :], d_sc[128 * oc:128 * (oc + 1), :], sq)
            e_qwld[oc] = prog.dma("gpsimd", sz_t[:, par, 1, :],
                                  d_zr[128 * oc:128 * (oc + 1), :], sq)
            e_szld[oc] = e_qwld[oc]
            e_wold[oc] = prog.dma("gpsimd", wo_bf[:, par, :, :],
                                  wo_view[:, :, 128 * oc:128 * (oc + 1)],
                                  "dWO0" if par == 0 else "dWO1",
                                  wait=gate + [e_mm_done.get(oc - 2)], tag=f"wo{oc}")

        n_oc = 2 if (stop_after or "").startswith("D2") else NI
        d_lvl = {"D2s": 0, "D2t": 1, "D2x": 2, "D2o": 3, "D2": 9}.get(stop_after, 9)
        prefetch(0)
        prefetch(1, qw_too=False)
        e_qw1 = {}
        assert True
        for oc in range(n_oc):
            par = oc % 2
            e_zs = prog.op("vector", lambda e, par=par: e.tensor_mul(
                zs_t[:, par, :], sz_t[:, par, 1, :], sz_t[:, par, 0, :]),
                wait=[e_szld[oc], e_qwld[oc], e_qw1.get(oc),
                      e_Wt_done.get(oc - 2)], inc="cV")
            e_Wev = None
            e_wn_prev = None
            for st in range(4):
                sp = st % 2
                s_bc = sz_t[:, par, 0, 16 * st:16 * (st + 1)].unsqueeze(2) \
                    .broadcast_to([128, 16, GS])
                zs_bc = zs_t[:, par, 16 * st:16 * (st + 1)].unsqueeze(2) \
                    .broadcast_to([128, 16, GS])
                q3 = qw_bf[:, 1024 * st:1024 * (st + 1)] \
                    .rearrange("p (g r) -> p g r", r=GS)
                w = [e_zs] if st == 0 else []
                w.append(ev.get(("wn_free", sp)))
                w.append(e_wn_prev)
                e_tt1 = prog.op("vector", lambda e, q3=q3, s_bc=s_bc: e.tensor_tensor(
                    qs_scr[:].rearrange("p (g r) -> p g r", r=GS), q3, s_bc, ALU.mult),
                    wait=w, inc="cV")
                e_wn = prog.op("vector", lambda e, sp=sp, zs_bc=zs_bc: e.tensor_tensor(
                    wn_strip[:, sp, :].rearrange("p (g r) -> p g r", r=GS),
                    qs_scr[:].rearrange("p (g r) -> p g r", r=GS), zs_bc, ALU.subtract),
                    wait=[e_tt1, e_zs], inc="cV")
                e_wn_prev = e_wn
                e_Wev = None
                for h2 in range(2):
                    pws = pWt[h2]
                    trl = None
                    for u in range(4):
                        w2 = [e_wn, ev.get(("pWt_free", h2))] if u == 0 else []
                        trl = prog.op("tensor", lambda e, sp=sp, u=u, h2=h2, pws=pws:
                                      e.matmul(pws[:, 128 * u:128 * (u + 1)],
                                               wn_strip[:, sp, 128 * (4 * h2 + u):
                                                        128 * (4 * h2 + u + 1)],
                                               ident_bf[:], start=True, stop=True),
                                      wait=w2, inc="cT" if u == 3 else None)
                    e_Wev = prog.op("scalar", lambda e, par=par, st=st, h2=h2, pws=pws:
                                    e.copy(Wt[:, par, 8 * st + 4 * h2:8 * st + 4 * h2 + 4, :],
                                           pws[:].rearrange("p (a b) -> p a b", b=128)),
                                    wait=[trl], inc="cS")
                    ev[("pWt_free", h2)] = e_Wev
                ev[("wn_free", sp)] = e_Wev
            e_Wt_done[oc] = e_Wev
            if oc + 1 < n_oc:
                sq1 = "dWQ0" if (oc + 1) % 2 == 0 else "dWQ1"
                e_qw1[oc + 1] = prog.dma(
                    "gpsimd", qw_bf[:], d_qw[128 * (oc + 1):128 * (oc + 2), :],
                    sq1, wait=[e_Wt_done[oc]], tag=f"qw{oc + 1}")
            mm = None
            for tc in range(2):
                for ic in range(NI):
                    w = []
                    if tc == 0 and ic == 0:
                        w = [e_Wt_done[oc], e_wold[oc], ev["attn_done"], ev["B_t"],
                             ev["u_done"], ev["xn2_done"], e_sfm.get(oc - 2)]
                    prog.op("tensor", lambda e, par=par, tc=tc, ic=ic: e.matmul(
                        pout[par][:, 512 * tc:512 * (tc + 1)],
                        Wt[:, par, ic, :].squeeze(1) if False else Wt[:, par, ic, :],
                        xn2[:, ic, 512 * tc:512 * (tc + 1)],
                        start=(ic == 0), stop=False), wait=w)
                    prog.op("tensor", lambda e, par=par, tc=tc, ic=ic: e.matmul(
                        pout[par][:, 512 * tc:512 * (tc + 1)], wo_bf[:, par, ic, :],
                        attn_own[:, ic, 512 * tc:512 * (tc + 1)],
                        start=False, stop=False))
                mm = prog.op("tensor", lambda e, par=par, tc=tc, oc=oc: e.matmul(
                    pout[par][:, 512 * tc:512 * (tc + 1)],
                    B_t[:, 128 * oc:128 * (oc + 1)],
                    u_sc[:, 512 * tc:512 * (tc + 1)], start=False, stop=True),
                    inc="cT" if tc == 1 else None)
            e_mm_done[oc] = mm
            prefetch(oc + 2, qw_too=False)
            e_sfm[oc] = prog.op("vector", lambda e, par=par, oc=oc: e.tensor_single_scalar(
                s_fm[:], pout[par][:], bias_t[:, oc:oc + 1], ALU.add),
                wait=[mm, ev["bias_t"], e_trlast.get(oc - 1)], inc="cV")
            if d_lvl == 0:
                e_trlast[oc] = e_sfm[oc]
                continue
            e_x = prog.dma("sync", xsl[:],
                           d_xo.rearrange("(j p) o -> p j o", p=128)
                           [:, :, 128 * oc:128 * (oc + 1)],
                           "dX0", wait=[ev.get("xsl_read")], tag=f"xsl{oc}")
            e_add = None
            for j in range(8):
                bk = (j // 4) % 2
                pts = pT[bk]
                w = [e_sfm[oc]] if j % 4 == 0 else []
                if j % 4 == 0:
                    w.append(ev.get(("pT_free", bk)))
                trj = prog.op("tensor", lambda e, j=j, pts=pts: e.matmul(
                    pts[:, 128 * (j % 4):128 * (j % 4) + 128],
                    s_fm[:, 128 * j:128 * (j + 1)], ident_bf[:],
                    start=True, stop=True), wait=w, inc="cT")
                if j == 7:
                    e_trlast[oc] = trj
                if d_lvl == 1:
                    ev[("pT_free", bk)] = trj
                    continue
                e_add = prog.op("vector", lambda e, j=j, pts=pts: e.tensor_add(
                    ostage[:, j, :], pts[:, 128 * (j % 4):128 * (j % 4) + 128],
                    xsl[:, j, :]),
                    wait=[trj, e_x, ev.get("ostage_free")], inc="cV")
                if j % 4 == 3:
                    ev[("pT_free", bk)] = e_add
            if d_lvl >= 2:
                ev["xsl_read"] = e_add
            if d_lvl in (2, 3):
                continue
            e_od = prog.dma("gpsimd",
                            d_out.rearrange("(j p) o -> p j o", p=128)
                            [:, :, 128 * oc:128 * (oc + 1)],
                            ostage[:], "dO0", wait=[e_add], tag=f"out{oc}")
            ev["ostage_free"] = e_od
        if d_lvl >= 9 or prog.cnt["dO0"] + prog.cnt["dO1"] > 0:
            prog.op("sync", None, wait=[("dO0", prog.cnt["dO0"]),
                                        ("dO1", prog.cnt["dO1"])])
            prog.op("gpsimd", None, wait=[("dO0", prog.cnt["dO0"]),
                                          ("dO1", prog.cnt["dO1"])])
        else:
            last = e_trlast.get(n_oc - 1) if d_lvl == 1 else None
            ws = [w for w in (e_sfm.get(n_oc - 1), last,
                              ev.get(("pT_free", 0)) if d_lvl == 2 else None) if w]
            e_bo = prog.dma("sync", d_out[0:128, 0:512],
                            recip_bc[:, 0:512] if False else s_fm[:, 0:1024].bitcast(F32)[:, 0:512],
                            "dO0", wait=ws)
            prog.op("sync", None, wait=[e_bo])

    for g in (g_xn2, g_attn):
        g.__exit__(None, None, None)
        open_guards.remove(g)

    return _finish(nc, es, prog, open_guards)


def _finish(nc, es, prog, open_guards=()):
    for g in reversed(list(open_guards)):
        g.__exit__(None, None, None)
    sems = {s: es.enter_context(nc.semaphore(s)) for s in SEMS}
    block = es.enter_context(nc.Block())
    watermark = {e: {s: 0 for s in SEMS} for e in ENGS}

    def emit_engine(eng_name):
        oplist = prog.ops[eng_name]
        wm = watermark[eng_name]

        def body(eng):
            for fn, waits, inc, tag in oplist:
                merged = {}
                for (sem, val) in waits:
                    merged[sem] = max(merged.get(sem, 0), val)
                for sem, val in merged.items():
                    if val > wm[sem]:
                        eng.wait_ge(sems[sem], val)
                        wm[sem] = val
                instr = fn(eng) if fn is not None else None
                if inc is not None:
                    sem, n = inc
                    instr.then_inc(sems[sem], n)
        return body

    for e in ENGS:
        getattr(block, e)(emit_engine(e))

    es.close()
    return nc


_NC_CACHE = None


def _get_nc():
    global _NC_CACHE
    if _NC_CACHE is None:
        _NC_CACHE = build_nc()
    return _NC_CACHE


def kernel(L0_attn_x, L0_attn_wq, L0_attn_wk, L0_attn_wv, L0_attn_wo,
           L0_ffn_qweight, L0_ffn_scales, L0_ffn_zeros, L0_ffn_lora_A,
           L0_ffn_lora_B, L0_ffn_bias):
    import ml_dtypes
    x = np.ascontiguousarray(np.asarray(L0_attn_x, dtype=np.float32))
    wq = np.ascontiguousarray(np.asarray(L0_attn_wq, dtype=np.float32))
    wk = np.ascontiguousarray(np.asarray(L0_attn_wk, dtype=np.float32))
    wv = np.ascontiguousarray(np.asarray(L0_attn_wv, dtype=np.float32))
    wo = np.ascontiguousarray(np.asarray(L0_attn_wo, dtype=np.float32))
    qw = np.ascontiguousarray(np.asarray(L0_ffn_qweight, dtype=np.int32))
    sc = np.ascontiguousarray(np.asarray(L0_ffn_scales, dtype=np.float32))
    zr = np.ascontiguousarray(np.asarray(L0_ffn_zeros, dtype=np.float32))
    lA = np.ascontiguousarray(np.asarray(L0_ffn_lora_A, dtype=np.float32))
    lB = np.ascontiguousarray(np.asarray(L0_ffn_lora_B, dtype=np.float32))
    bias = np.ascontiguousarray(np.asarray(L0_ffn_bias, dtype=np.float32))

    ident_bf = np.eye(128, dtype=ml_dtypes.bfloat16)
    ident_f = np.eye(128, dtype=np.float32)

    nc = _get_nc()
    in_maps = []
    for c in range(8):
        b, a = c // 4, c % 4
        p = np.arange(128)[None, :, None]
        f = np.arange(128)[None, None, :]
        jj = np.arange(4)[:, None, None]
        m01 = (4 * f + a >= 128 * jj + p).astype(ml_dtypes.bfloat16)
        in_maps.append({
            "x_b": x[b],
            "x_own": np.ascontiguousarray(x[b, a::4, :]),
            "wq": wq, "wk": wk, "wv": wv, "wo": wo,
            "qweight": qw, "scales": sc, "zeros": zr,
            "lora_A": lA, "lora_B": lB, "bias": bias,
            "ident_bf": ident_bf, "ident_f": ident_f,
            "masks01": np.ascontiguousarray(m01),
        })
    res = run_bass_kernel_spmd(nc, in_maps, core_ids=list(range(8)))
    out = np.empty((2, T, DM), dtype=np.float32)
    for c in range(8):
        b, a = c // 4, c % 4
        out[b, a::4, :] = res.results[c]["out_own"]
    return out


# version 07 (final)
# speedup vs baseline: 1.3164x; 1.3164x over previous
import math
from contextlib import ExitStack

import numpy as np
import concourse.bass as bass
import concourse.mybir as mybir
from concourse.bass_utils import run_bass_kernel_spmd

F32 = mybir.dt.float32
BF16 = mybir.dt.bfloat16
I32 = mybir.dt.int32
AF = mybir.ActivationFunctionType
ALU = mybir.AluOpType
AX = mybir.AxisListType

T = 4096
DM = 4096
H = 32
DH = 128
NI = 32
NOWN = 1024
NSTRIP = T // 128
NOSTRIP = NOWN // 128
EPS = 1e-5
ASCALE = 1.0 / math.sqrt(DH)
K0 = -0.5 * math.log(DH)
GS = 64
NG = DM // GS
RANK = 16
LORA_SC = 1.0 / RANK

ENGS = ("sync", "gpsimd", "tensor", "vector", "scalar")
SEMS = ("dPS", "dPG", "dX0", "dX1", "dWQ0", "dWQ1", "dWO0", "dWO1",
        "dO0", "dO1", "cT", "cV", "cS")


class Prog:
    def __init__(self):
        self.ops = {e: [] for e in ENGS}
        self.cnt = {s: 0 for s in SEMS}

    def op(self, eng, fn, wait=(), inc=None, n=1, tag=""):
        waits = [w for w in wait if w is not None]
        evt = None
        if inc is not None:
            self.cnt[inc] += n
            evt = (inc, self.cnt[inc])
        self.ops[eng].append((fn, waits, (inc, n) if inc else None, tag))
        return evt

    def dma(self, eng, out_ap, in_ap, sem, wait=(), tag=""):
        return self.op(eng, lambda e, o=out_ap, i=in_ap: e.dma_start(out=o, in_=i),
                       wait=wait, inc=sem, n=16, tag=tag)


LAST_PROG = None


def build_nc(stop_after=None):
    global LAST_PROG
    nc = bass.Bass("TRN2", target_bir_lowering=False)

    d_x = nc.dram_tensor("x_b", [T, DM], F32, kind="ExternalInput").ap()
    d_xo = nc.dram_tensor("x_own", [NOWN, DM], F32, kind="ExternalInput").ap()
    d_wq = nc.dram_tensor("wq", [DM, DM], F32, kind="ExternalInput").ap()
    d_wk = nc.dram_tensor("wk", [DM, DH], F32, kind="ExternalInput").ap()
    d_wv = nc.dram_tensor("wv", [DM, DH], F32, kind="ExternalInput").ap()
    d_wo = nc.dram_tensor("wo", [DM, DM], F32, kind="ExternalInput").ap()
    d_qw = nc.dram_tensor("qweight", [DM, DM], I32, kind="ExternalInput").ap()
    d_sc = nc.dram_tensor("scales", [DM, NG], F32, kind="ExternalInput").ap()
    d_zr = nc.dram_tensor("zeros", [DM, NG], F32, kind="ExternalInput").ap()
    d_lA = nc.dram_tensor("lora_A", [RANK, DM], F32, kind="ExternalInput").ap()
    d_lB = nc.dram_tensor("lora_B", [DM, RANK], F32, kind="ExternalInput").ap()
    d_bias = nc.dram_tensor("bias", [DM], F32, kind="ExternalInput").ap()
    d_idb = nc.dram_tensor("ident_bf", [128, 128], BF16, kind="ExternalInput").ap()
    d_idf = nc.dram_tensor("ident_f", [128, 128], F32, kind="ExternalInput").ap()
    d_msk = nc.dram_tensor("masks01", [4, 128, 128], BF16, kind="ExternalInput").ap()
    d_out = nc.dram_tensor("out_own", [NOWN, DM], F32, kind="ExternalOutput").ap()

    prog = Prog()
    LAST_PROG = prog
    es = ExitStack()
    ev = {}
    open_guards = []

    def sb(name, shape, dtype, side="left"):
        return es.enter_context(nc.sbuf_tensor(name, shape, dtype, side=side))

    ones_col = sb("ones_col", [128, 1], BF16)
    ones_row = sb("ones_row", [1, 128], BF16)
    eps_t = sb("eps_t", [128, 1], F32)
    k0_t = sb("k0_t", [1, 1], F32)
    ident_bf = sb("ident_bf_t", [128, 128], BF16)
    ident_f = sb("ident_f_t", [128, 128], F32)
    sums_t = sb("sums_t", [128, 2], F32)
    sumsq_t = sb("sumsq_t", [128, 2], F32)
    mean_t = sb("mean_t", [128, 2], F32)
    m2_t = sb("m2_t", [128, 2], F32)
    m2e_t = sb("m2e_t", [128, 2], F32)
    rstd_t = sb("rstd_t", [128, 2], F32)
    qm_all = sb("qm_all", [1, 32], F32)
    kmx_t = sb("kmx_t", [1, 8], F32)
    km2_t = sb("km2_t", [1, 1], F32)
    t32_t = sb("t32_t", [1, 32], F32)
    w32_t = sb("w32_t", [1, 32], F32)
    brow_t = sb("brow_t", [1, 32], BF16)
    bias_c = sb("bias_c", [128, 32], F32)
    A_t = sb("A_t", [128, NI * RANK], BF16)
    B_t = sb("B_t", [16, DM], BF16)
    u_sc = sb("u_sc", [16, NOWN], BF16)
    bias_t = sb("bias_t", [128, 32], F32)
    masks = sb("masks_t", [128, 4, 128], BF16)
    scale_col = sb("scale_col", [128, 1], F32)
    mean_own = sb("mean_own", [128, NOSTRIP], F32)
    rstd_own = sb("rstd_own", [128, NOSTRIP], F32)

    prog.dma("sync", ident_bf[:], d_idb, "dPS")
    prog.dma("sync", ident_f[:], d_idf, "dPS")
    for j in range(4):
        e_consts = prog.dma("sync", masks[:, j, :], d_msk[j], "dPS")
    ev["consts"] = e_consts
    e_ones = prog.op("vector", lambda e: e.memset(ones_col[:], 1.0), inc="cV")
    prog.op("vector", lambda e: e.memset(ones_row[:], 1.0), inc="cV")
    prog.op("vector", lambda e: e.memset(eps_t[:], EPS), inc="cV")
    e_k0 = prog.op("vector", lambda e: e.memset(k0_t[:], K0), inc="cV")
    e_c128 = prog.op("vector", lambda e: e.memset(scale_col[:], 128.0), inc="cV")
    e_sl = prog.op("scalar", lambda e: e.activation(scale_col[:], scale_col[:], AF.Ln,
                                                    bias=0.0, scale=1.0),
                   wait=[e_c128], inc="cS")
    e_scol = prog.op("scalar", lambda e: e.activation(scale_col[:], scale_col[:],
                                                      AF.Exp, bias=0.0, scale=-0.5),
                     wait=[e_sl], inc="cS")
    ev["scale_col"] = e_scol


    def bisect_finish(tile_ap, wait_ev):
        e_bo = prog.dma("sync", d_out[0:128, :], tile_ap, "dO0", wait=[wait_ev])
        prog.op("sync", None, wait=[e_bo])

    def make_ln(xstrip, xn_nat, ptrs, key):
        def ln_strip(src_dram, row0, s, dst3, col0, prev, save_idx=None):
            eA_prev, eR_prev, eT_prev = prev
            p = s % 2
            e_ld = prog.dma("sync", xstrip[:, p, :], src_dram[row0:row0 + 128, :],
                            "dX0" if p == 0 else "dX1",
                            wait=[eA_prev], tag=f"{key}xld{s}")
            e_sum = prog.op("vector", lambda e, p=p: e.reduce_sum(
                sums_t[:, p:p + 1], xstrip[:, p, :], axis=AX.X),
                wait=[e_ld, eR_prev], inc="cV")
            e_sq = prog.op("scalar", lambda e, p=p: e.activation(
                xn_nat[:, p, :], xstrip[:, p, :], AF.Square,
                accum_out=sumsq_t[:, p:p + 1]), wait=[e_ld, eT_prev], inc="cS")
            e_mean = prog.op("scalar", lambda e, p=p: e.mul(
                mean_t[:, p:p + 1], sums_t[:, p:p + 1], 1.0 / DM),
                wait=[e_sum, eA_prev], inc="cS")
            e_m2 = prog.op("vector", lambda e, p=p: e.tensor_mul(
                m2_t[:, p:p + 1], mean_t[:, p:p + 1], mean_t[:, p:p + 1]),
                wait=[e_mean], inc="cV")
            e_m2e = prog.op("vector", lambda e, p=p: e.tensor_scalar(
                m2e_t[:, p:p + 1], m2_t[:, p:p + 1], -1.0, EPS, ALU.mult, ALU.add),
                wait=[e_m2], inc="cV")
            e_ln = prog.op("scalar", lambda e, p=p: e.activation(
                rstd_t[:, p:p + 1], sumsq_t[:, p:p + 1], AF.Ln,
                bias=m2e_t[:, p:p + 1], scale=1.0 / DM),
                wait=[e_m2e, e_sq], inc="cS")
            e_rstd = prog.op("scalar", lambda e, p=p: e.activation(
                rstd_t[:, p:p + 1], rstd_t[:, p:p + 1], AF.Exp, bias=0.0, scale=-0.5),
                wait=[e_ln], inc="cS")
            e_apply = prog.op("vector", lambda e, p=p: e.tensor_scalar(
                xn_nat[:, p, :], xstrip[:, p, :], mean_t[:, p:p + 1],
                rstd_t[:, p:p + 1], ALU.subtract, ALU.mult),
                wait=[e_rstd], inc="cV")
            if save_idx is not None:
                prog.op("scalar", lambda e, p=p, si=save_idx: e.copy(
                    mean_own[:, si:si + 1], mean_t[:, p:p + 1]), wait=[e_rstd])
                prog.op("scalar", lambda e, p=p, si=save_idx: e.copy(
                    rstd_own[:, si:si + 1], rstd_t[:, p:p + 1]))
            e_evac = None
            tr_last = None
            for j in range(8):
                pj = ptrs[j % 2]
                for u in range(4):
                    ic = 4 * j + u
                    w = []
                    if u == 0:
                        w = [e_apply, ev.get((key, "ptr_free", j % 2))]
                    tr_last = prog.op("tensor", lambda e, p=p, ic=ic, u=u, pj=pj:
                                      e.matmul(pj[:, 128 * u:128 * (u + 1)],
                                               xn_nat[:, p, 128 * ic:128 * (ic + 1)],
                                               ident_bf[:], start=True, stop=True),
                                      wait=w, inc="cT" if u == 3 else None)
                e_evac = prog.op("scalar", lambda e, j=j, pj=pj, dst3=dst3, c0=col0:
                                 e.copy(dst3[:, 4 * j:4 * (j + 1), c0:c0 + 128],
                                        pj[:].rearrange("p (a b) -> p a b", b=128)),
                                 wait=[tr_last], inc="cS")
                ev[(key, "ptr_free", j % 2)] = e_evac
            return (e_apply, e_rstd, tr_last), e_evac
        return ln_strip

    g_K = nc.sbuf_tensor("K_fm", [128, T], BF16)
    K_fm = g_K.__enter__()
    open_guards.append(g_K)
    g_V = nc.sbuf_tensor("V_tok", [128, NI, 128], BF16)
    V_tok = g_V.__enter__()
    open_guards.append(g_V)

    g_xn = nc.sbuf_tensor("xn_own", [128, NI, NOWN], BF16, side="right")
    xn_own = g_xn.__enter__()
    open_guards.append(g_xn)

    with nc.sbuf_tensor("wk_bf", [128, NI, DH], BF16) as wk_bf, \
         nc.sbuf_tensor("wv_bf", [128, NI, DH], BF16) as wv_bf, \
         nc.sbuf_tensor("xstrip", [128, 2, DM], F32) as xstrip, \
         nc.sbuf_tensor("xn_nat", [128, 2, DM], BF16) as xn_nat, \
         nc.sbuf_tensor("cbuf", [128, NI, 512], BF16) as cbuf, \
         nc.sbuf_tensor("vfm_scr", [128, 512], BF16) as vfm_scr, \
         nc.psum_tensor("ptr_a", [128, 512], F32) as ptr_a, \
         nc.psum_tensor("ptr_b", [128, 512], F32) as ptr_b, \
         nc.psum_tensor("pK", [128, 512], F32) as pK, \
         nc.psum_tensor("pV", [128, 512], F32) as pV:

        ev[("A", "ptr_free", 0)] = ev["consts"]
        ev[("A", "ptr_free", 1)] = ev["consts"]
        ln_strip = make_ln(xstrip, xn_nat, [ptr_a, ptr_b], "A")
        prog.dma("gpsimd", wk_bf[:], d_wk.rearrange("(c p) d -> p c d", p=128), "dPG")
        e_wv = prog.dma("gpsimd", wv_bf[:], d_wv.rearrange("(c p) d -> p c d", p=128),
                        "dPG")
        e_wk = e_wv

        prev = (None, None, None)
        e_Kev = None
        for C in range(8):
            last_evac = None
            for r in range(4):
                s = 4 * C + r
                prev, evc = ln_strip(d_x, 128 * s, s, cbuf, 128 * r, prev)
                last_evac = evc
            mmK = None
            for ic in range(NI):
                w = [last_evac, e_wk, e_Kev] if ic == 0 else []
                mmK = prog.op("tensor", lambda e, ic=ic: e.matmul(
                    pK[:], wk_bf[:, ic, :], cbuf[:, ic, :],
                    start=(ic == 0), stop=(ic == NI - 1)),
                    wait=w, inc="cT" if ic == NI - 1 else None)
            mmV = None
            for ic in range(NI):
                w = [e_wv] if ic == 0 else []
                mmV = prog.op("tensor", lambda e, ic=ic: e.matmul(
                    pV[:], wv_bf[:, ic, :], cbuf[:, ic, :],
                    start=(ic == 0), stop=(ic == NI - 1)),
                    wait=w, inc="cT" if ic == NI - 1 else None)
            e_Kev = prog.op("scalar", lambda e, C=C: e.copy(
                K_fm[:, 512 * C:512 * (C + 1)], pK[:]), wait=[mmK], inc="cS")
            e_Vev = prog.op("scalar", lambda e: e.copy(vfm_scr[:], pV[:]),
                            wait=[mmV], inc="cS")
            tr = None
            for r2 in range(4):
                w = [e_Vev]
                if r2 == 0:
                    w.append(ev.get(("A", "ptr_free", 0)))
                tr = prog.op("tensor", lambda e, r2=r2: e.matmul(
                    ptr_a[:, 128 * r2:128 * (r2 + 1)],
                    vfm_scr[:, 128 * r2:128 * (r2 + 1)], ident_bf[:],
                    start=True, stop=True),
                    wait=w, inc="cT" if r2 == 3 else None)
            e_e2 = prog.op("scalar", lambda e, C=C: e.copy(
                V_tok[:, 4 * C:4 * C + 4, :],
                ptr_a[:].rearrange("p (a b) -> p a b", b=128)),
                wait=[tr], inc="cS")
            ev[("A", "ptr_free", 0)] = e_e2

        e_xn = None
        for s in range(NOSTRIP):
            prev, evc = ln_strip(d_xo, 128 * s, NSTRIP + s, xn_own, 128 * s, prev,
                                 save_idx=s)
            e_xn = evc
        ev["xn_done"] = e_xn
        ev["a_prev"] = prev
        if stop_after == "A":
            bisect_finish(xstrip[:, 0, :], ev["xn_done"])
            ev["early"] = True

    if stop_after == "A":
        return _finish(nc, es, prog, open_guards)
    g_q = nc.sbuf_tensor("q_own", [128, H, NOWN], BF16)
    q_own = g_q.__enter__()
    open_guards.append(g_q)

    with nc.sbuf_tensor("wq_bf", [128, 2, NI, 128], BF16) as wq_bf, \
         nc.sbuf_tensor("qsq_scr", [128, NOWN], BF16) as qsq_scr, \
         nc.sbuf_tensor("ksq_scr", [128, T], BF16) as ksq_scr, \
         nc.sbuf_tensor("lA_bf", [16, DM], BF16) as lA_bf, \
         nc.psum_tensor("pq_a", [128, 1024], F32) as pq_a, \
         nc.psum_tensor("pq_b", [128, 1024], F32) as pq_b, \
         nc.psum_tensor("pqsq", [128, 1024], F32) as pqsq, \
         nc.psum_tensor("pAt", [128, 512], F32) as pAt:

        pq = [pq_a, pq_b]
        wq_view = d_wq.rearrange("(c p) o -> p c o", p=128)
        e_wq = {}
        for d0 in range(2):
            e_wq[d0] = prog.dma("gpsimd", wq_bf[:, d0, :, :],
                                wq_view[:, :, 128 * d0:128 * (d0 + 1)],
                                "dWQ0" if d0 % 2 == 0 else "dWQ1",
                                wait=[ev["xn_done"]], tag=f"wq{d0}")
        e_lA = prog.dma("gpsimd", lA_bf[:], d_lA, "dPG", wait=[ev["xn_done"]])

        e_qev = {}
        e_red = {}
        e_qsqmm = {}
        for dout in range(H):
            par = dout % 2
            mm_last = None
            for tc in range(2):
                for ic in range(NI):
                    w = []
                    if tc == 0 and ic == 0:
                        w = [e_wq[dout], ev["xn_done"], e_qev.get(dout - 2)]
                    mm_last = prog.op("tensor", lambda e, par=par, tc=tc, ic=ic:
                                      e.matmul(pq[par][:, 512 * tc:512 * (tc + 1)],
                                               wq_bf[:, par, ic, :],
                                               xn_own[:, ic, 512 * tc:512 * (tc + 1)],
                                               start=(ic == 0), stop=(ic == NI - 1)),
                                      wait=w,
                                      inc="cT" if (tc == 1 and ic == NI - 1) else None)
            e_qev[dout] = prog.op("scalar", lambda e, par=par, dout=dout: e.copy(
                q_own[:, dout, :], pq[par][:]), wait=[mm_last], inc="cS")
            if dout + 2 < H:
                e_wq[dout + 2] = prog.dma(
                    "gpsimd", wq_bf[:, par, :, :],
                    wq_view[:, :, 128 * (dout + 2):128 * (dout + 3)],
                    "dWQ0" if par == 0 else "dWQ1",
                    wait=[e_qev[dout]], tag=f"wq{dout + 2}")
            e_sqv = prog.op("vector", lambda e, dout=dout: e.tensor_mul(
                qsq_scr[:], q_own[:, dout, :], q_own[:, dout, :]),
                wait=[e_qev[dout], e_qsqmm.get(dout - 1)], inc="cV")
            mm2 = None
            for tc in range(2):
                w = [e_sqv, e_ones, e_red.get(dout - 1)] if tc == 0 else []
                mm2 = prog.op("tensor", lambda e, tc=tc: e.matmul(
                    pqsq[0:1, 512 * tc:512 * (tc + 1)], ones_col[:],
                    qsq_scr[:, 512 * tc:512 * (tc + 1)], start=True, stop=True),
                    wait=w, inc="cT" if tc == 1 else None)
            e_qsqmm[dout] = mm2
            e_red[dout] = prog.op("vector", lambda e, dout=dout: e.reduce_max(
                qm_all[0:1, dout:dout + 1], pqsq[0:1, 0:1024], axis=AX.X),
                wait=[mm2], inc="cV")

        e_ks = prog.op("vector", lambda e: e.tensor_mul(ksq_scr[:], K_fm[:], K_fm[:]),
                       inc="cV")
        e_prev_red = e_red[H - 1]
        for C in range(8):
            mmk = prog.op("tensor", lambda e, C=C: e.matmul(
                pqsq[0:1, 0:512], ones_col[:], ksq_scr[:, 512 * C:512 * (C + 1)],
                start=True, stop=True), wait=[e_ks, e_prev_red], inc="cT")
            e_prev_red = prog.op("vector", lambda e, C=C: e.reduce_max(
                kmx_t[0:1, C:C + 1], pqsq[0:1, 0:512], axis=AX.X),
                wait=[mmk], inc="cV")
        e_sp = prog.op("scalar", lambda e: e.copy(km2_t[:], kmx_t[0:1, 0:1]),
                       wait=[e_prev_red], inc="cS")
        e_km2 = prog.op("vector", lambda e: e.reduce_max(km2_t[:], kmx_t[:], axis=AX.X),
                        wait=[e_sp, e_prev_red], inc="cV")
        e_c1 = prog.op("scalar", lambda e: e.mul(t32_t[:], qm_all[:], km2_t[0:1, 0:1]),
                       wait=[e_km2, e_k0], inc="cS")
        e_c2 = prog.op("scalar", lambda e: e.activation(t32_t[:], t32_t[:], AF.Ln,
                                                        bias=0.0, scale=1.0),
                       wait=[e_c1], inc="cS")
        e_c3 = prog.op("scalar", lambda e: e.activation(w32_t[:], t32_t[:], AF.Exp,
                                                        bias=k0_t[0:1, 0:1], scale=0.5),
                       wait=[e_c2], inc="cS")
        e_brow = prog.op("scalar", lambda e: e.mul(brow_t[:], w32_t[:], -1.0),
                         wait=[e_c3], inc="cS")
        e_bcm = prog.op("tensor", lambda e: e.matmul(
            pqsq[:, 0:32], ones_row[:], brow_t[:], start=True, stop=True),
            wait=[e_brow], inc="cT")
        ev["bias_c"] = prog.op("scalar", lambda e: e.copy(bias_c[:], pqsq[:, 0:32]),
                               wait=[e_bcm], inc="cS")

        A3 = A_t[:].rearrange("p (a b) -> p a b", b=RANK)
        tr = None
        for ic in range(NI):
            tr = prog.op("tensor", lambda e, ic=ic: e.matmul(
                pAt[:, RANK * ic:RANK * (ic + 1)], lA_bf[:, 128 * ic:128 * (ic + 1)],
                ident_bf[0:16, 0:16], start=True, stop=True),
                wait=[e_lA] if ic == 0 else [],
                inc="cT" if ic == NI - 1 else None)
        e_Atev = prog.op("scalar", lambda e: e.copy(A_t[:, 0:NI * RANK], pAt[:, 0:NI * RANK]),
                         wait=[tr], inc="cS")
        e_udone = None
        for tc in range(2):
            mmu = None
            for ic in range(NI):
                w = [e_Atev] if ic == 0 and tc == 0 else []
                mmu = prog.op("tensor", lambda e, tc=tc, ic=ic, A3=A3: e.matmul(
                    pq_a[0:16, 512 * tc:512 * (tc + 1)], A3[:, ic, :],
                    xn_own[:, ic, 512 * tc:512 * (tc + 1)],
                    start=(ic == 0), stop=(ic == NI - 1)),
                    wait=w, inc="cT" if ic == NI - 1 else None)
            e_udone = prog.op("vector", lambda e, tc=tc: e.tensor_single_scalar(
                u_sc[:, 512 * tc:512 * (tc + 1)], pq_a[0:16, 512 * tc:512 * (tc + 1)],
                LORA_SC, ALU.mult), wait=[mmu], inc="cV")
        ev["u_done"] = e_udone
        if stop_after == "B":
            e_bo = prog.dma("sync", d_out[0:128, 0:2048],
                            ksq_scr[:, 0:4096].bitcast(F32), "dO0",
                            wait=[ev["u_done"]])
            prog.op("sync", None, wait=[e_bo])

    if stop_after == "B":
        return _finish(nc, es, prog, open_guards)
    g_xn.__exit__(None, None, None)
    open_guards.remove(g_xn)
    g_attn = nc.sbuf_tensor("attn_own", [128, NI, NOWN], BF16, side="right")
    attn_own = g_attn.__enter__()
    open_guards.append(g_attn)

    with nc.sbuf_tensor("probsT", [128, 3, 1024], BF16) as probsT, \
         nc.sbuf_tensor("zeros_bf", [128, 512], BF16) as zeros_bf, \
         nc.sbuf_tensor("recip_sb", [1, 512], F32) as recip_sb, \
         nc.sbuf_tensor("recip_bf", [1, 512], BF16) as recip_bf, \
         nc.sbuf_tensor("recip_bc", [128, 512], F32) as recip_bc, \
         nc.psum_tensor("pS_a", [128, 1024], F32) as pS_a, \
         nc.psum_tensor("pS_b", [128, 1024], F32) as pS_b, \
         nc.psum_tensor("pAV", [128, 512], F32) as pAV, \
         nc.psum_tensor("psums", [128, 512], F32) as psums, \
         nc.psum_tensor("pbc", [128, 512], F32) as pbc:

        pS = [pS_a, pS_b]
        e_ms0 = prog.op("vector", lambda e: e.memset(pS_a[:], 0.0),
                        wait=[ev["u_done"]], inc="cV")
        e_ms1 = prog.op("vector", lambda e: e.memset(pS_b[:], 0.0),
                        wait=[ev["u_done"]], inc="cV")
        e_zero = prog.op("vector", lambda e: e.memset(zeros_bf[:], 0.0),
                         wait=[ev["u_done"]], inc="cV")
        eS_free = {0: e_ms0, 1: e_ms1}
        eP_free = {0: None, 1: None, 2: None}
        e_avread = None
        e_sumread = None
        pslot_i = 0
        for h in range(H):
            for beta in range(2):
                nkc = 16 * (beta + 1)
                e_pair_last = None
                pend = None

                def emit_av(pair):
                    nonlocal e_pair_last
                    kc0p, ps_p, w_p, gate_p = pair
                    last = None
                    for kk in range(2):
                        kc = kc0p + kk
                        qoff = 512 - w_p
                        w = []
                        if kk == 0:
                            w = [gate_p]
                            if kc == 0:
                                w += [e_avread, e_sumread]
                        prog.op("tensor", lambda e, kc=kc, ps_i=ps_p, kk=kk,
                                w_kc=w_p, qoff=qoff: e.matmul(
                            pAV[:, qoff:512], V_tok[:, kc, :],
                            probsT[:, ps_i, 512 * kk:512 * kk + w_kc],
                            start=(kc == 0), stop=False), wait=w)
                        last = prog.op("tensor", lambda e, kc=kc, ps_i=ps_p, kk=kk,
                                       w_kc=w_p, qoff=qoff: e.matmul(
                            psums[0:1, qoff:512], ones_col[:],
                            probsT[:, ps_i, 512 * kk:512 * kk + w_kc],
                            start=(kc == 0), stop=False), inc="cT")
                    eP_free[ps_p] = last
                    e_pair_last = last

                for kc0 in range(0, nkc, 2):
                    slot = (kc0 // 2) % 2
                    w_kc = 512 - 128 * max(0, kc0 // 4 - 4 * beta)
                    off = 512 * beta + (512 - w_kc)
                    mmS = None
                    for kk in range(2):
                        w = []
                        if kk == 0:
                            w = [eS_free[slot]]
                            if h == 0 and beta == 0 and kc0 == 0:
                                w.append(ev["bias_c"])
                        kc = kc0 + kk
                        mmS = prog.op("tensor", lambda e, slot=slot, kk=kk, kc=kc,
                                      w_kc=w_kc, h=h, off=off: e.matmul(
                            pS[slot][:, 512 * kk:512 * kk + w_kc],
                            K_fm[:, 128 * kc:128 * (kc + 1)],
                            q_own[:, h, off:off + w_kc],
                            start=True, stop=True),
                            wait=w, inc="cT" if kk == 1 else None)
                    ps_i = pslot_i % 3
                    pslot_i += 1
                    e_exp = prog.op("scalar", lambda e, slot=slot, ps_i=ps_i,
                                    w_kc=w_kc, h=h: e.activation(
                        probsT[:, ps_i, 0:512 + w_kc], pS[slot][:, 0:512 + w_kc],
                        AF.Exp, bias=bias_c[:, h:h + 1], scale=scale_col[:]),
                        wait=[mmS, eP_free[ps_i],
                              ev["scale_col"] if (h == 0 and beta == 0) else None],
                        inc="cS")
                    eS_free[slot] = e_exp
                    e_gate = e_exp
                    if kc0 // 4 >= 4 * beta:
                        for kk in range(2):
                            j = (kc0 + kk) % 4
                            e_gate = prog.op("vector", lambda e, ps_i=ps_i, kk=kk,
                                             w_kc=w_kc, j=j: e.tensor_mul(
                                probsT[:, ps_i, 512 * kk:512 * kk + 128],
                                probsT[:, ps_i, 512 * kk:512 * kk + 128],
                                masks[:, j, :]),
                                wait=[e_exp, ev["consts"]] if kk == 0 else [],
                                inc="cV" if kk == 1 else None)
                    if pend is not None:
                        emit_av(pend)
                    pend = (kc0, ps_i, w_kc, e_gate)
                emit_av(pend)
                prog.op("tensor", lambda e: e.matmul(
                    pAV[:], V_tok[:, 0, :], zeros_bf[:], start=False, stop=True),
                    wait=[e_zero])
                e_pair_last = prog.op("tensor", lambda e: e.matmul(
                    psums[0:1, :], ones_col[:], zeros_bf[:], start=False, stop=True),
                    inc="cT")
                e_rc = prog.op("vector", lambda e: e.reciprocal(
                    recip_sb[:], psums[0:1, :]), wait=[e_pair_last], inc="cV")
                e_sumread = e_rc
                e_rcp = prog.op("scalar", lambda e: e.copy(recip_bf[:], recip_sb[:]),
                                wait=[e_rc], inc="cS")
                e_bcm2 = prog.op("tensor", lambda e: e.matmul(
                    pbc[:], ones_row[:], recip_bf[:], start=True, stop=True),
                    wait=[e_rcp], inc="cT")
                e_bce = prog.op("scalar", lambda e: e.copy(recip_bc[:], pbc[:]),
                                wait=[e_bcm2], inc="cS")
                e_avread = prog.op("vector", lambda e, h=h, beta=beta: e.tensor_mul(
                    attn_own[:, h, 512 * beta:512 * (beta + 1)], pAV[:], recip_bc[:]),
                    wait=[e_bce], inc="cV")
        ev["attn_done"] = e_avread
        if stop_after == "C":
            e_bo = prog.dma("sync", d_out[0:128, 0:512], recip_bc[:, 0:512],
                            "dO0", wait=[ev["attn_done"]])
            prog.op("sync", None, wait=[e_bo])

    if stop_after == "C":
        return _finish(nc, es, prog, open_guards)

    for g in (g_q, g_V, g_K):
        g.__exit__(None, None, None)
        open_guards.remove(g)

    g_xn2 = nc.sbuf_tensor("xn_own2", [128, NI, NOWN], BF16, side="right")
    xn2 = g_xn2.__enter__()
    open_guards.append(g_xn2)

    with nc.sbuf_tensor("xstrip2", [128, 2, DM], F32) as xstrip2, \
         nc.sbuf_tensor("xn_nat2", [128, 2, DM], BF16) as xn_nat2, \
         nc.sbuf_tensor("b32_scr", [32, 128], BF16) as b32_scr, \
         nc.sbuf_tensor("lB_bf", [128, NI, RANK], BF16) as lB_bf, \
         nc.psum_tensor("ptr2_a", [128, 512], F32) as ptr2_a, \
         nc.psum_tensor("ptr2_b", [128, 512], F32) as ptr2_b, \
         nc.psum_tensor("pmisc", [128, 512], F32) as pmisc, \
         nc.psum_tensor("pmiscb", [128, 512], F32) as pmiscb:

        ptr2 = [ptr2_a, ptr2_b]
        eA_prev = ev["attn_done"]
        eT_prev2 = ev["a_prev"][2]
        evc = None
        nreb = 0 if stop_after == "D1b" else (2 if stop_after in ("D1c", "D1d") else NOSTRIP)
        skip_tr = stop_after == "D1d"
        for s in range(nreb):
            p = s % 2
            e_ld = prog.dma("sync", xstrip2[:, p, :], d_xo[128 * s:128 * (s + 1), :],
                            "dX0" if p == 0 else "dX1", wait=[eA_prev],
                            tag=f"Dxld{s}")
            e_apply = prog.op("vector", lambda e, p=p, s=s: e.tensor_scalar(
                xn_nat2[:, p, :], xstrip2[:, p, :], mean_own[:, s:s + 1],
                rstd_own[:, s:s + 1], ALU.subtract, ALU.mult),
                wait=[e_ld, eT_prev2], inc="cV")
            eA_prev = e_apply
            if skip_tr:
                evc = e_apply
                continue
            tr_last = None
            for j in range(8):
                pj = ptr2[j % 2]
                for u in range(4):
                    ic = 4 * j + u
                    w = []
                    if u == 0:
                        w = [e_apply, ev.get(("D", "ptr_free", j % 2))]
                    tr_last = prog.op("tensor", lambda e, p=p, ic=ic, u=u, pj=pj:
                                      e.matmul(pj[:, 128 * u:128 * (u + 1)],
                                               xn_nat2[:, p, 128 * ic:128 * (ic + 1)],
                                               ident_bf[:], start=True, stop=True),
                                      wait=w, inc="cT" if u == 3 else None)
                evc = prog.op("scalar", lambda e, j=j, pj=pj, s=s: e.copy(
                    xn2[:, 4 * j:4 * (j + 1), 128 * s:128 * s + 128],
                    pj[:].rearrange("p (a b) -> p a b", b=128)),
                    wait=[tr_last], inc="cS")
                ev[("D", "ptr_free", j % 2)] = evc
            eT_prev2 = tr_last
        ev["xn2_done"] = evc if evc is not None else ev["attn_done"]
        if stop_after in ("D1c", "D1d"):
            e_bo = prog.dma("sync", d_out[0:128, 0:1024],
                            xn_nat2[:, 0, :].bitcast(F32)[:, 0:1024], "dO0",
                            wait=[ev["xn2_done"]])
            prog.op("sync", None, wait=[e_bo])
        if stop_after == "D1a":
            e_bo = prog.dma("sync", d_out[0:128, 0:1024],
                            xn_nat2[:, 0, :].bitcast(F32)[:, 0:1024], "dO0",
                            wait=[ev["xn2_done"]])
            prog.op("sync", None, wait=[e_bo])
        if stop_after == "D1c":
            pass

        prog.dma("gpsimd", b32_scr[:],
                 d_bias.rearrange("(c p) -> c p", p=128), "dPG",
                 wait=[ev["attn_done"]])
        e_lB = prog.dma("gpsimd", lB_bf[:], d_lB.rearrange("(c p) r -> p c r", p=128),
                        "dPG")
        trb = prog.op("tensor", lambda e: e.matmul(
            pmisc[:, 0:32], b32_scr[:], ident_bf[0:32, 0:32],
            start=True, stop=True), wait=[e_lB], inc="cT")
        ev["bias_t"] = prog.op("scalar", lambda e: e.copy(bias_t[:], pmisc[:, 0:32]),
                               wait=[trb], inc="cS")
        e_Bt = {}
        for g in range(NI // 4):
            trc = None
            for u in range(4):
                cc = 4 * g + u
                w = []
                if u == 0:
                    w = [e_lB, e_Bt.get(g - 1)]
                trc = prog.op("tensor", lambda e, cc=cc, u=u: e.matmul(
                    pmiscb[0:16, 128 * u:128 * (u + 1)], lB_bf[:, cc, :],
                    ident_bf[:], start=True, stop=True),
                    wait=w, inc="cT" if u == 3 else None)
            e_Bt[g] = prog.op("scalar", lambda e, g=g: e.copy(
                B_t[:, 512 * g:512 * (g + 1)], pmiscb[0:16, 0:512]),
                wait=[trc], inc="cS")
        ev["B_t"] = e_Bt[NI // 4 - 1]
        if stop_after == "D1b":
            e_bo = prog.dma("sync", d_out[0:16, 0:512],
                            B_t[0:16, 0:1024].bitcast(F32),
                            "dO0", wait=[ev["B_t"], ev["bias_t"]])
            prog.op("sync", None, wait=[e_bo])
        if stop_after == "D1":
            e_bo = prog.dma("sync", d_out[0:128, 0:1024],
                            xn_nat2[:, 0, :].bitcast(F32)[:, 0:1024], "dO0",
                            wait=[ev["xn2_done"], ev["B_t"], ev["bias_t"]])
            prog.op("sync", None, wait=[e_bo])

    if stop_after in ("D1", "D1a", "D1b", "D1c", "D1d"):
        return _finish(nc, es, prog, open_guards)

    with nc.sbuf_tensor("qw_bf", [128, DM], BF16) as qw_bf, \
         nc.sbuf_tensor("qs_scr", [128, 1024], F32) as qs_scr, \
         nc.sbuf_tensor("wn_strip", [128, 2, 1024], BF16) as wn_strip, \
         nc.sbuf_tensor("Wt", [128, 2, NI, 128], BF16) as Wt, \
         nc.sbuf_tensor("wo_bf", [128, 2, NI, 128], BF16) as wo_bf, \
         nc.sbuf_tensor("sz_t", [128, 2, 2, GS], F32) as sz_t, \
         nc.sbuf_tensor("zs_t", [128, 2, GS], F32) as zs_t, \
         nc.sbuf_tensor("s_fm", [128, 1024], BF16) as s_fm, \
         nc.sbuf_tensor("ostage", [128, 8, 128], F32) as ostage, \
         nc.sbuf_tensor("xsl", [128, 8, 128], F32) as xsl, \
         nc.psum_tensor("pout_a", [128, 1024], F32) as pout_a, \
         nc.psum_tensor("pout_b", [128, 1024], F32) as pout_b, \
         nc.psum_tensor("pWt_a", [128, 512], F32) as pWt_a, \
         nc.psum_tensor("pWt_b", [128, 512], F32) as pWt_b, \
         nc.psum_tensor("pT_a", [128, 512], F32) as pT_a, \
         nc.psum_tensor("pT_b", [128, 512], F32) as pT_b:

        pout = [pout_a, pout_b]
        pWt = [pWt_a, pWt_b]
        pT = [pT_a, pT_b]
        wo_view = d_wo.rearrange("(c p) o -> p c o", p=128)
        e_qwld = {}
        e_szld = {}
        e_wold = {}
        e_Wt_done = {}
        e_mm_done = {}
        e_sfm = {}
        e_trlast = {}

        def prefetch(oc, qw_too=True):
            if oc >= NI:
                return
            par = oc % 2
            gate = ([ev["attn_done"], ev["xn2_done"], ev["B_t"], ev["bias_t"]]
                    if oc < 2 else [])
            sq = "dWQ0" if par == 0 else "dWQ1"
            if qw_too:
                prog.dma("gpsimd", qw_bf[:], d_qw[128 * oc:128 * (oc + 1), :],
                         sq, wait=gate + [e_Wt_done.get(oc - 1)], tag=f"qw{oc}")
            prog.dma("gpsimd", sz_t[:, par, 0, :], d_sc[128 * oc:128 * (oc + 1), :], sq)
            e_qwld[oc] = prog.dma("gpsimd", sz_t[:, par, 1, :],
                                  d_zr[128 * oc:128 * (oc + 1), :], sq)
            e_szld[oc] = e_qwld[oc]
            e_wold[oc] = prog.dma("gpsimd", wo_bf[:, par, :, :],
                                  wo_view[:, :, 128 * oc:128 * (oc + 1)],
                                  "dWO0" if par == 0 else "dWO1",
                                  wait=gate + [e_mm_done.get(oc - 2)], tag=f"wo{oc}")

        n_oc = 2 if (stop_after or "").startswith("D2") else NI
        d_lvl = {"D2s": 0, "D2t": 1, "D2x": 2, "D2o": 3, "D2": 9}.get(stop_after, 9)
        prefetch(0)
        prefetch(1, qw_too=False)
        e_qw1 = {}
        assert True
        for oc in range(n_oc):
            par = oc % 2
            e_zs = prog.op("vector", lambda e, par=par: e.tensor_mul(
                zs_t[:, par, :], sz_t[:, par, 1, :], sz_t[:, par, 0, :]),
                wait=[e_szld[oc], e_qwld[oc], e_qw1.get(oc),
                      e_Wt_done.get(oc - 2)], inc="cV")
            e_Wev = None
            e_wn_prev = None
            for st in range(4):
                sp = st % 2
                s_bc = sz_t[:, par, 0, 16 * st:16 * (st + 1)].unsqueeze(2) \
                    .broadcast_to([128, 16, GS])
                zs_bc = zs_t[:, par, 16 * st:16 * (st + 1)].unsqueeze(2) \
                    .broadcast_to([128, 16, GS])
                q3 = qw_bf[:, 1024 * st:1024 * (st + 1)] \
                    .rearrange("p (g r) -> p g r", r=GS)
                w = [e_zs] if st == 0 else []
                w.append(ev.get(("wn_free", sp)))
                w.append(e_wn_prev)
                e_tt1 = prog.op("vector", lambda e, q3=q3, s_bc=s_bc: e.tensor_tensor(
                    qs_scr[:].rearrange("p (g r) -> p g r", r=GS), q3, s_bc, ALU.mult),
                    wait=w, inc="cV")
                e_wn = prog.op("vector", lambda e, sp=sp, zs_bc=zs_bc: e.tensor_tensor(
                    wn_strip[:, sp, :].rearrange("p (g r) -> p g r", r=GS),
                    qs_scr[:].rearrange("p (g r) -> p g r", r=GS), zs_bc, ALU.subtract),
                    wait=[e_tt1, e_zs], inc="cV")
                e_wn_prev = e_wn
                e_Wev = None
                for h2 in range(2):
                    pws = pWt[h2]
                    trl = None
                    for u in range(4):
                        w2 = [e_wn, ev.get(("pWt_free", h2))] if u == 0 else []
                        trl = prog.op("tensor", lambda e, sp=sp, u=u, h2=h2, pws=pws:
                                      e.matmul(pws[:, 128 * u:128 * (u + 1)],
                                               wn_strip[:, sp, 128 * (4 * h2 + u):
                                                        128 * (4 * h2 + u + 1)],
                                               ident_bf[:], start=True, stop=True),
                                      wait=w2, inc="cT" if u == 3 else None)
                    e_Wev = prog.op("scalar", lambda e, par=par, st=st, h2=h2, pws=pws:
                                    e.copy(Wt[:, par, 8 * st + 4 * h2:8 * st + 4 * h2 + 4, :],
                                           pws[:].rearrange("p (a b) -> p a b", b=128)),
                                    wait=[trl], inc="cS")
                    ev[("pWt_free", h2)] = e_Wev
                ev[("wn_free", sp)] = e_Wev
            e_Wt_done[oc] = e_Wev
            if oc + 1 < n_oc:
                sq1 = "dWQ0" if (oc + 1) % 2 == 0 else "dWQ1"
                e_qw1[oc + 1] = prog.dma(
                    "gpsimd", qw_bf[:], d_qw[128 * (oc + 1):128 * (oc + 2), :],
                    sq1, wait=[e_Wt_done[oc]], tag=f"qw{oc + 1}")
            mm = None
            for tc in range(2):
                for ic in range(NI):
                    w = []
                    if tc == 0 and ic == 0:
                        w = [e_Wt_done[oc], e_wold[oc], ev["attn_done"], ev["B_t"],
                             ev["u_done"], ev["xn2_done"], e_sfm.get(oc - 2)]
                    prog.op("tensor", lambda e, par=par, tc=tc, ic=ic: e.matmul(
                        pout[par][:, 512 * tc:512 * (tc + 1)],
                        Wt[:, par, ic, :].squeeze(1) if False else Wt[:, par, ic, :],
                        xn2[:, ic, 512 * tc:512 * (tc + 1)],
                        start=(ic == 0), stop=False), wait=w)
                    prog.op("tensor", lambda e, par=par, tc=tc, ic=ic: e.matmul(
                        pout[par][:, 512 * tc:512 * (tc + 1)], wo_bf[:, par, ic, :],
                        attn_own[:, ic, 512 * tc:512 * (tc + 1)],
                        start=False, stop=False))
                mm = prog.op("tensor", lambda e, par=par, tc=tc, oc=oc: e.matmul(
                    pout[par][:, 512 * tc:512 * (tc + 1)],
                    B_t[:, 128 * oc:128 * (oc + 1)],
                    u_sc[:, 512 * tc:512 * (tc + 1)], start=False, stop=True),
                    inc="cT" if tc == 1 else None)
            e_mm_done[oc] = mm
            prefetch(oc + 2, qw_too=False)
            e_sfm[oc] = prog.op("vector", lambda e, par=par, oc=oc: e.tensor_single_scalar(
                s_fm[:], pout[par][:], bias_t[:, oc:oc + 1], ALU.add),
                wait=[mm, ev["bias_t"], e_trlast.get(oc - 1)], inc="cV")
            if d_lvl == 0:
                e_trlast[oc] = e_sfm[oc]
                continue
            e_x = prog.dma("sync", xsl[:],
                           d_xo.rearrange("(j p) o -> p j o", p=128)
                           [:, :, 128 * oc:128 * (oc + 1)],
                           "dX0", wait=[ev.get("xsl_read")], tag=f"xsl{oc}")
            e_add = None
            for j in range(8):
                bk = (j // 4) % 2
                pts = pT[bk]
                w = [e_sfm[oc]] if j % 4 == 0 else []
                if j % 4 == 0:
                    w.append(ev.get(("pT_free", bk)))
                trj = prog.op("tensor", lambda e, j=j, pts=pts: e.matmul(
                    pts[:, 128 * (j % 4):128 * (j % 4) + 128],
                    s_fm[:, 128 * j:128 * (j + 1)], ident_bf[:],
                    start=True, stop=True), wait=w, inc="cT")
                if j == 7:
                    e_trlast[oc] = trj
                if d_lvl == 1:
                    ev[("pT_free", bk)] = trj
                    continue
                e_add = prog.op("vector", lambda e, j=j, pts=pts: e.tensor_add(
                    ostage[:, j, :], pts[:, 128 * (j % 4):128 * (j % 4) + 128],
                    xsl[:, j, :]),
                    wait=[trj, e_x, ev.get("ostage_free")], inc="cV")
                if j % 4 == 3:
                    ev[("pT_free", bk)] = e_add
            if d_lvl >= 2:
                ev["xsl_read"] = e_add
            if d_lvl in (2, 3):
                continue
            e_od = prog.dma("gpsimd",
                            d_out.rearrange("(j p) o -> p j o", p=128)
                            [:, :, 128 * oc:128 * (oc + 1)],
                            ostage[:], "dO0", wait=[e_add], tag=f"out{oc}")
            ev["ostage_free"] = e_od
        if d_lvl >= 9 or prog.cnt["dO0"] + prog.cnt["dO1"] > 0:
            prog.op("sync", None, wait=[("dO0", prog.cnt["dO0"]),
                                        ("dO1", prog.cnt["dO1"])])
            prog.op("gpsimd", None, wait=[("dO0", prog.cnt["dO0"]),
                                          ("dO1", prog.cnt["dO1"])])
        else:
            last = e_trlast.get(n_oc - 1) if d_lvl == 1 else None
            ws = [w for w in (e_sfm.get(n_oc - 1), last,
                              ev.get(("pT_free", 0)) if d_lvl == 2 else None) if w]
            e_bo = prog.dma("sync", d_out[0:128, 0:512],
                            recip_bc[:, 0:512] if False else s_fm[:, 0:1024].bitcast(F32)[:, 0:512],
                            "dO0", wait=ws)
            prog.op("sync", None, wait=[e_bo])

    for g in (g_xn2, g_attn):
        g.__exit__(None, None, None)
        open_guards.remove(g)

    return _finish(nc, es, prog, open_guards)


def _finish(nc, es, prog, open_guards=()):
    for g in reversed(list(open_guards)):
        g.__exit__(None, None, None)
    sems = {s: es.enter_context(nc.semaphore(s)) for s in SEMS}
    block = es.enter_context(nc.Block())
    watermark = {e: {s: 0 for s in SEMS} for e in ENGS}

    def emit_engine(eng_name):
        oplist = prog.ops[eng_name]
        wm = watermark[eng_name]

        def body(eng):
            for fn, waits, inc, tag in oplist:
                merged = {}
                for (sem, val) in waits:
                    merged[sem] = max(merged.get(sem, 0), val)
                for sem, val in merged.items():
                    if val > wm[sem]:
                        eng.wait_ge(sems[sem], val)
                        wm[sem] = val
                instr = fn(eng) if fn is not None else None
                if inc is not None:
                    sem, n = inc
                    instr.then_inc(sems[sem], n)
        return body

    for e in ENGS:
        getattr(block, e)(emit_engine(e))

    es.close()
    return nc


_NC_CACHE = None


def _get_nc():
    global _NC_CACHE
    if _NC_CACHE is None:
        _NC_CACHE = build_nc()
    return _NC_CACHE


def kernel(L0_attn_x, L0_attn_wq, L0_attn_wk, L0_attn_wv, L0_attn_wo,
           L0_ffn_qweight, L0_ffn_scales, L0_ffn_zeros, L0_ffn_lora_A,
           L0_ffn_lora_B, L0_ffn_bias):
    import ml_dtypes
    x = np.ascontiguousarray(np.asarray(L0_attn_x, dtype=np.float32))
    wq = np.ascontiguousarray(np.asarray(L0_attn_wq, dtype=np.float32))
    wk = np.ascontiguousarray(np.asarray(L0_attn_wk, dtype=np.float32))
    wv = np.ascontiguousarray(np.asarray(L0_attn_wv, dtype=np.float32))
    wo = np.ascontiguousarray(np.asarray(L0_attn_wo, dtype=np.float32))
    qw = np.ascontiguousarray(np.asarray(L0_ffn_qweight, dtype=np.int32))
    sc = np.ascontiguousarray(np.asarray(L0_ffn_scales, dtype=np.float32))
    zr = np.ascontiguousarray(np.asarray(L0_ffn_zeros, dtype=np.float32))
    lA = np.ascontiguousarray(np.asarray(L0_ffn_lora_A, dtype=np.float32))
    lB = np.ascontiguousarray(np.asarray(L0_ffn_lora_B, dtype=np.float32))
    bias = np.ascontiguousarray(np.asarray(L0_ffn_bias, dtype=np.float32))

    ident_bf = np.eye(128, dtype=ml_dtypes.bfloat16)
    ident_f = np.eye(128, dtype=np.float32)

    nc = _get_nc()
    in_maps = []
    for c in range(8):
        b, a = c // 4, c % 4
        p = np.arange(128)[None, :, None]
        f = np.arange(128)[None, None, :]
        jj = np.arange(4)[:, None, None]
        m01 = (4 * f + a >= 128 * jj + p).astype(ml_dtypes.bfloat16)
        in_maps.append({
            "x_b": x[b],
            "x_own": np.ascontiguousarray(x[b, a::4, :]),
            "wq": wq, "wk": wk, "wv": wv, "wo": wo,
            "qweight": qw, "scales": sc, "zeros": zr,
            "lora_A": lA, "lora_B": lB, "bias": bias,
            "ident_bf": ident_bf, "ident_f": ident_f,
            "masks01": np.ascontiguousarray(m01),
        })
    res = run_bass_kernel_spmd(nc, in_maps, core_ids=list(range(8)))
    out = np.empty((2, T, DM), dtype=np.float32)
    for c in range(8):
        b, a = c // 4, c % 4
        out[b, a::4, :] = res.results[c]["out_own"]
    return out


# version 08
# speedup vs baseline: 1.3394x; 1.0175x over previous
import math
from contextlib import ExitStack

import numpy as np
import concourse.bass as bass
import concourse.mybir as mybir
from concourse.bass_utils import run_bass_kernel_spmd

F32 = mybir.dt.float32
BF16 = mybir.dt.bfloat16
I32 = mybir.dt.int32
AF = mybir.ActivationFunctionType
ALU = mybir.AluOpType
AX = mybir.AxisListType

T = 4096
DM = 4096
H = 32
DH = 128
NI = 32
NOWN = 1024
NSTRIP = T // 128
NOSTRIP = NOWN // 128
EPS = 1e-5
ASCALE = 1.0 / math.sqrt(DH)
K0 = -0.5 * math.log(DH)
GS = 64
NG = DM // GS
RANK = 16
LORA_SC = 1.0 / RANK

ENGS = ("sync", "gpsimd", "tensor", "vector", "scalar")
SEMS = ("dPS", "dPG", "dX0", "dX1", "dWQ0", "dWQ1", "dWO0", "dWO1",
        "dO0", "dO1", "cT", "cV", "cS")


class Prog:
    def __init__(self):
        self.ops = {e: [] for e in ENGS}
        self.cnt = {s: 0 for s in SEMS}

    def op(self, eng, fn, wait=(), inc=None, n=1, tag=""):
        waits = [w for w in wait if w is not None]
        evt = None
        if inc is not None:
            self.cnt[inc] += n
            evt = (inc, self.cnt[inc])
        self.ops[eng].append((fn, waits, (inc, n) if inc else None, tag))
        return evt

    def dma(self, eng, out_ap, in_ap, sem, wait=(), tag=""):
        return self.op(eng, lambda e, o=out_ap, i=in_ap: e.dma_start(out=o, in_=i),
                       wait=wait, inc=sem, n=16, tag=tag)


LAST_PROG = None


def build_nc(stop_after=None):
    global LAST_PROG
    nc = bass.Bass("TRN2", target_bir_lowering=False)

    d_x = nc.dram_tensor("x_b", [T, DM], F32, kind="ExternalInput").ap()
    d_xo = nc.dram_tensor("x_own", [NOWN, DM], F32, kind="ExternalInput").ap()
    d_wq = nc.dram_tensor("wq", [DM, DM], F32, kind="ExternalInput").ap()
    d_wk = nc.dram_tensor("wk", [DM, DH], F32, kind="ExternalInput").ap()
    d_wv = nc.dram_tensor("wv", [DM, DH], F32, kind="ExternalInput").ap()
    d_wo = nc.dram_tensor("wo", [DM, DM], F32, kind="ExternalInput").ap()
    d_qw = nc.dram_tensor("qweight", [DM, DM], I32, kind="ExternalInput").ap()
    d_sc = nc.dram_tensor("scales", [DM, NG], F32, kind="ExternalInput").ap()
    d_zr = nc.dram_tensor("zeros", [DM, NG], F32, kind="ExternalInput").ap()
    d_lA = nc.dram_tensor("lora_A", [RANK, DM], F32, kind="ExternalInput").ap()
    d_lB = nc.dram_tensor("lora_B", [DM, RANK], F32, kind="ExternalInput").ap()
    d_bias = nc.dram_tensor("bias", [DM], F32, kind="ExternalInput").ap()
    d_idb = nc.dram_tensor("ident_bf", [128, 128], BF16, kind="ExternalInput").ap()
    d_idf = nc.dram_tensor("ident_f", [128, 128], F32, kind="ExternalInput").ap()
    d_msk = nc.dram_tensor("masks01", [4, 128, 128], BF16, kind="ExternalInput").ap()
    d_out = nc.dram_tensor("out_own", [NOWN, DM], F32, kind="ExternalOutput").ap()

    prog = Prog()
    LAST_PROG = prog
    es = ExitStack()
    ev = {}
    open_guards = []

    def sb(name, shape, dtype, side="left"):
        return es.enter_context(nc.sbuf_tensor(name, shape, dtype, side=side))

    ones_col = sb("ones_col", [128, 1], BF16)
    ones_row = sb("ones_row", [1, 128], BF16)
    eps_t = sb("eps_t", [128, 1], F32)
    k0_t = sb("k0_t", [1, 1], F32)
    ident_bf = sb("ident_bf_t", [128, 128], BF16)
    ident_f = sb("ident_f_t", [128, 128], F32)
    sums_t = sb("sums_t", [128, 2], F32)
    sumsq_t = sb("sumsq_t", [128, 2], F32)
    mean_t = sb("mean_t", [128, 2], F32)
    m2_t = sb("m2_t", [128, 2], F32)
    m2e_t = sb("m2e_t", [128, 2], F32)
    rstd_t = sb("rstd_t", [128, 2], F32)
    qm_all = sb("qm_all", [1, 32], F32)
    kmx_t = sb("kmx_t", [1, 8], F32)
    km2_t = sb("km2_t", [1, 1], F32)
    t32_t = sb("t32_t", [1, 32], F32)
    w32_t = sb("w32_t", [1, 32], F32)
    brow_t = sb("brow_t", [1, 32], BF16)
    bias_c = sb("bias_c", [128, 32], F32)
    A_t = sb("A_t", [128, NI * RANK], BF16)
    B_t = sb("B_t", [16, DM], BF16)
    u_sc = sb("u_sc", [16, NOWN], BF16)
    bias_t = sb("bias_t", [128, 32], F32)
    masks = sb("masks_t", [128, 4, 128], BF16)
    scale_col = sb("scale_col", [128, 1], F32)
    mean_own = sb("mean_own", [128, NOSTRIP], F32)
    rstd_own = sb("rstd_own", [128, NOSTRIP], F32)

    prog.dma("sync", ident_bf[:], d_idb, "dPS")
    prog.dma("sync", ident_f[:], d_idf, "dPS")
    for j in range(4):
        e_consts = prog.dma("sync", masks[:, j, :], d_msk[j], "dPS")
    ev["consts"] = e_consts
    e_ones = prog.op("vector", lambda e: e.memset(ones_col[:], 1.0), inc="cV")
    prog.op("vector", lambda e: e.memset(ones_row[:], 1.0), inc="cV")
    prog.op("vector", lambda e: e.memset(eps_t[:], EPS), inc="cV")
    e_k0 = prog.op("vector", lambda e: e.memset(k0_t[:], K0), inc="cV")
    e_c128 = prog.op("vector", lambda e: e.memset(scale_col[:], 128.0), inc="cV")
    e_sl = prog.op("scalar", lambda e: e.activation(scale_col[:], scale_col[:], AF.Ln,
                                                    bias=0.0, scale=1.0),
                   wait=[e_c128], inc="cS")
    e_scol = prog.op("scalar", lambda e: e.activation(scale_col[:], scale_col[:],
                                                      AF.Exp, bias=0.0, scale=-0.5),
                     wait=[e_sl], inc="cS")
    ev["scale_col"] = e_scol


    def bisect_finish(tile_ap, wait_ev):
        e_bo = prog.dma("sync", d_out[0:128, :], tile_ap, "dO0", wait=[wait_ev])
        prog.op("sync", None, wait=[e_bo])

    def make_ln(xstrip, xn_nat, ptrs, key):
        def ln_strip(src_dram, row0, s, dst3, col0, prev, save_idx=None):
            eA_prev, eR_prev, eT_prev = prev
            p = s % 2
            e_ld = prog.dma("sync", xstrip[:, p, :], src_dram[row0:row0 + 128, :],
                            "dX0" if p == 0 else "dX1",
                            wait=[eA_prev], tag=f"{key}xld{s}")
            e_sum = prog.op("vector", lambda e, p=p: e.reduce_sum(
                sums_t[:, p:p + 1], xstrip[:, p, :], axis=AX.X),
                wait=[e_ld, eR_prev], inc="cV")
            e_sq = prog.op("scalar", lambda e, p=p: e.activation(
                xn_nat[:, p, :], xstrip[:, p, :], AF.Square,
                accum_out=sumsq_t[:, p:p + 1]), wait=[e_ld, eT_prev], inc="cS")
            e_mean = prog.op("scalar", lambda e, p=p: e.mul(
                mean_t[:, p:p + 1], sums_t[:, p:p + 1], 1.0 / DM),
                wait=[e_sum, eA_prev], inc="cS")
            e_m2 = prog.op("vector", lambda e, p=p: e.tensor_mul(
                m2_t[:, p:p + 1], mean_t[:, p:p + 1], mean_t[:, p:p + 1]),
                wait=[e_mean], inc="cV")
            e_m2e = prog.op("vector", lambda e, p=p: e.tensor_scalar(
                m2e_t[:, p:p + 1], m2_t[:, p:p + 1], -1.0, EPS, ALU.mult, ALU.add),
                wait=[e_m2], inc="cV")
            e_ln = prog.op("scalar", lambda e, p=p: e.activation(
                rstd_t[:, p:p + 1], sumsq_t[:, p:p + 1], AF.Ln,
                bias=m2e_t[:, p:p + 1], scale=1.0 / DM),
                wait=[e_m2e, e_sq], inc="cS")
            e_rstd = prog.op("scalar", lambda e, p=p: e.activation(
                rstd_t[:, p:p + 1], rstd_t[:, p:p + 1], AF.Exp, bias=0.0, scale=-0.5),
                wait=[e_ln], inc="cS")
            e_apply = prog.op("vector", lambda e, p=p: e.tensor_scalar(
                xn_nat[:, p, :], xstrip[:, p, :], mean_t[:, p:p + 1],
                rstd_t[:, p:p + 1], ALU.subtract, ALU.mult),
                wait=[e_rstd], inc="cV")
            if save_idx is not None:
                prog.op("scalar", lambda e, p=p, si=save_idx: e.copy(
                    mean_own[:, si:si + 1], mean_t[:, p:p + 1]), wait=[e_rstd])
                prog.op("scalar", lambda e, p=p, si=save_idx: e.copy(
                    rstd_own[:, si:si + 1], rstd_t[:, p:p + 1]))
            e_evac = None
            tr_last = None
            for j in range(8):
                pj = ptrs[j % 2]
                for u in range(4):
                    ic = 4 * j + u
                    w = []
                    if u == 0:
                        w = [e_apply, ev.get((key, "ptr_free", j % 2))]
                    tr_last = prog.op("tensor", lambda e, p=p, ic=ic, u=u, pj=pj:
                                      e.matmul(pj[:, 128 * u:128 * (u + 1)],
                                               xn_nat[:, p, 128 * ic:128 * (ic + 1)],
                                               ident_bf[:], start=True, stop=True),
                                      wait=w, inc="cT" if u == 3 else None)
                e_evac = prog.op("scalar", lambda e, j=j, pj=pj, dst3=dst3, c0=col0:
                                 e.copy(dst3[:, 4 * j:4 * (j + 1), c0:c0 + 128],
                                        pj[:].rearrange("p (a b) -> p a b", b=128)),
                                 wait=[tr_last], inc="cS")
                ev[(key, "ptr_free", j % 2)] = e_evac
            return (e_apply, e_rstd, tr_last), e_evac
        return ln_strip

    g_K = nc.sbuf_tensor("K_fm", [128, T], BF16)
    K_fm = g_K.__enter__()
    open_guards.append(g_K)
    g_V = nc.sbuf_tensor("V_tok", [128, NI, 128], BF16)
    V_tok = g_V.__enter__()
    open_guards.append(g_V)

    g_xn = nc.sbuf_tensor("xn_own", [128, NI, NOWN], BF16, side="right")
    xn_own = g_xn.__enter__()
    open_guards.append(g_xn)

    with nc.sbuf_tensor("wk_bf", [128, NI, DH], BF16) as wk_bf, \
         nc.sbuf_tensor("wv_bf", [128, NI, DH], BF16) as wv_bf, \
         nc.sbuf_tensor("xstrip", [128, 2, DM], F32) as xstrip, \
         nc.sbuf_tensor("xn_nat", [128, 2, DM], BF16) as xn_nat, \
         nc.sbuf_tensor("cbuf", [128, NI, 512], BF16) as cbuf, \
         nc.sbuf_tensor("vfm_scr", [128, 512], BF16) as vfm_scr, \
         nc.psum_tensor("ptr_a", [128, 512], F32) as ptr_a, \
         nc.psum_tensor("ptr_b", [128, 512], F32) as ptr_b, \
         nc.psum_tensor("pK", [128, 512], F32) as pK, \
         nc.psum_tensor("pV", [128, 512], F32) as pV:

        ev[("A", "ptr_free", 0)] = ev["consts"]
        ev[("A", "ptr_free", 1)] = ev["consts"]
        ln_strip = make_ln(xstrip, xn_nat, [ptr_a, ptr_b], "A")
        prog.dma("gpsimd", wk_bf[:], d_wk.rearrange("(c p) d -> p c d", p=128), "dPG")
        e_wv = prog.dma("gpsimd", wv_bf[:], d_wv.rearrange("(c p) d -> p c d", p=128),
                        "dPG")
        e_wk = e_wv

        prev = (None, None, None)
        e_Kev = None
        for C in range(8):
            last_evac = None
            for r in range(4):
                s = 4 * C + r
                prev, evc = ln_strip(d_x, 128 * s, s, cbuf, 128 * r, prev)
                last_evac = evc
            mmK = None
            for ic in range(NI):
                w = [last_evac, e_wk, e_Kev] if ic == 0 else []
                mmK = prog.op("tensor", lambda e, ic=ic: e.matmul(
                    pK[:], wk_bf[:, ic, :], cbuf[:, ic, :],
                    start=(ic == 0), stop=(ic == NI - 1)),
                    wait=w, inc="cT" if ic == NI - 1 else None)
            mmV = None
            for ic in range(NI):
                w = [e_wv] if ic == 0 else []
                mmV = prog.op("tensor", lambda e, ic=ic: e.matmul(
                    pV[:], wv_bf[:, ic, :], cbuf[:, ic, :],
                    start=(ic == 0), stop=(ic == NI - 1)),
                    wait=w, inc="cT" if ic == NI - 1 else None)
            e_Kev = prog.op("scalar", lambda e, C=C: e.copy(
                K_fm[:, 512 * C:512 * (C + 1)], pK[:]), wait=[mmK], inc="cS")
            e_Vev = prog.op("scalar", lambda e: e.copy(vfm_scr[:], pV[:]),
                            wait=[mmV], inc="cS")
            tr = None
            for r2 in range(4):
                w = [e_Vev]
                if r2 == 0:
                    w.append(ev.get(("A", "ptr_free", 0)))
                tr = prog.op("tensor", lambda e, r2=r2: e.matmul(
                    ptr_a[:, 128 * r2:128 * (r2 + 1)],
                    vfm_scr[:, 128 * r2:128 * (r2 + 1)], ident_bf[:],
                    start=True, stop=True),
                    wait=w, inc="cT" if r2 == 3 else None)
            e_e2 = prog.op("scalar", lambda e, C=C: e.copy(
                V_tok[:, 4 * C:4 * C + 4, :],
                ptr_a[:].rearrange("p (a b) -> p a b", b=128)),
                wait=[tr], inc="cS")
            ev[("A", "ptr_free", 0)] = e_e2

        e_xn = None
        for s in range(NOSTRIP):
            prev, evc = ln_strip(d_xo, 128 * s, NSTRIP + s, xn_own, 128 * s, prev,
                                 save_idx=s)
            e_xn = evc
        ev["xn_done"] = e_xn
        ev["a_prev"] = prev
        if stop_after == "A":
            bisect_finish(xstrip[:, 0, :], ev["xn_done"])
            ev["early"] = True

    if stop_after == "A":
        return _finish(nc, es, prog, open_guards)
    g_q = nc.sbuf_tensor("q_own", [128, H, NOWN], BF16)
    q_own = g_q.__enter__()
    open_guards.append(g_q)

    with nc.sbuf_tensor("wq_bf", [128, 2, NI, 128], BF16) as wq_bf, \
         nc.sbuf_tensor("qsq_scr", [128, NOWN], BF16) as qsq_scr, \
         nc.sbuf_tensor("ksq_scr", [128, T], BF16) as ksq_scr, \
         nc.sbuf_tensor("lA_bf", [16, DM], BF16) as lA_bf, \
         nc.psum_tensor("pq_a", [128, 1024], F32) as pq_a, \
         nc.psum_tensor("pq_b", [128, 1024], F32) as pq_b, \
         nc.psum_tensor("pqsq", [128, 1024], F32) as pqsq, \
         nc.psum_tensor("pAt", [128, 512], F32) as pAt:

        pq = [pq_a, pq_b]
        wq_view = d_wq.rearrange("(c p) o -> p c o", p=128)
        e_wq = {}
        for d0 in range(2):
            e_wq[d0] = prog.dma("gpsimd", wq_bf[:, d0, :, :],
                                wq_view[:, :, 128 * d0:128 * (d0 + 1)],
                                "dWQ0" if d0 % 2 == 0 else "dWQ1",
                                wait=[ev["xn_done"]], tag=f"wq{d0}")
        e_lA = prog.dma("gpsimd", lA_bf[:], d_lA, "dPG", wait=[ev["xn_done"]])

        e_qev = {}
        e_red = {}
        e_qsqmm = {}
        for dout in range(H):
            par = dout % 2
            mm_last = None
            for tc in range(2):
                for ic in range(NI):
                    w = []
                    if tc == 0 and ic == 0:
                        w = [e_wq[dout], ev["xn_done"], e_qev.get(dout - 2)]
                    mm_last = prog.op("tensor", lambda e, par=par, tc=tc, ic=ic:
                                      e.matmul(pq[par][:, 512 * tc:512 * (tc + 1)],
                                               wq_bf[:, par, ic, :],
                                               xn_own[:, ic, 512 * tc:512 * (tc + 1)],
                                               start=(ic == 0), stop=(ic == NI - 1)),
                                      wait=w,
                                      inc="cT" if (tc == 1 and ic == NI - 1) else None)
            e_qev[dout] = prog.op("scalar", lambda e, par=par, dout=dout: e.copy(
                q_own[:, dout, :], pq[par][:]), wait=[mm_last], inc="cS")
            if dout + 2 < H:
                e_wq[dout + 2] = prog.dma(
                    "gpsimd", wq_bf[:, par, :, :],
                    wq_view[:, :, 128 * (dout + 2):128 * (dout + 3)],
                    "dWQ0" if par == 0 else "dWQ1",
                    wait=[e_qev[dout]], tag=f"wq{dout + 2}")
            e_sqv = prog.op("vector", lambda e, dout=dout: e.tensor_mul(
                qsq_scr[:], q_own[:, dout, :], q_own[:, dout, :]),
                wait=[e_qev[dout], e_qsqmm.get(dout - 1)], inc="cV")
            mm2 = None
            for tc in range(2):
                w = [e_sqv, e_ones, e_red.get(dout - 1)] if tc == 0 else []
                mm2 = prog.op("tensor", lambda e, tc=tc: e.matmul(
                    pqsq[0:1, 512 * tc:512 * (tc + 1)], ones_col[:],
                    qsq_scr[:, 512 * tc:512 * (tc + 1)], start=True, stop=True),
                    wait=w, inc="cT" if tc == 1 else None)
            e_qsqmm[dout] = mm2
            e_red[dout] = prog.op("vector", lambda e, dout=dout: e.reduce_max(
                qm_all[0:1, dout:dout + 1], pqsq[0:1, 0:1024], axis=AX.X),
                wait=[mm2], inc="cV")

        e_ks = prog.op("vector", lambda e: e.tensor_mul(ksq_scr[:], K_fm[:], K_fm[:]),
                       inc="cV")
        e_prev_red = e_red[H - 1]
        for C in range(8):
            mmk = prog.op("tensor", lambda e, C=C: e.matmul(
                pqsq[0:1, 0:512], ones_col[:], ksq_scr[:, 512 * C:512 * (C + 1)],
                start=True, stop=True), wait=[e_ks, e_prev_red], inc="cT")
            e_prev_red = prog.op("vector", lambda e, C=C: e.reduce_max(
                kmx_t[0:1, C:C + 1], pqsq[0:1, 0:512], axis=AX.X),
                wait=[mmk], inc="cV")
        e_sp = prog.op("scalar", lambda e: e.copy(km2_t[:], kmx_t[0:1, 0:1]),
                       wait=[e_prev_red], inc="cS")
        e_km2 = prog.op("vector", lambda e: e.reduce_max(km2_t[:], kmx_t[:], axis=AX.X),
                        wait=[e_sp, e_prev_red], inc="cV")
        e_c1 = prog.op("scalar", lambda e: e.mul(t32_t[:], qm_all[:], km2_t[0:1, 0:1]),
                       wait=[e_km2, e_k0], inc="cS")
        e_c2 = prog.op("scalar", lambda e: e.activation(t32_t[:], t32_t[:], AF.Ln,
                                                        bias=0.0, scale=1.0),
                       wait=[e_c1], inc="cS")
        e_c3 = prog.op("scalar", lambda e: e.activation(w32_t[:], t32_t[:], AF.Exp,
                                                        bias=k0_t[0:1, 0:1], scale=0.5),
                       wait=[e_c2], inc="cS")
        e_brow = prog.op("scalar", lambda e: e.mul(brow_t[:], w32_t[:], -1.0),
                         wait=[e_c3], inc="cS")
        e_bcm = prog.op("tensor", lambda e: e.matmul(
            pqsq[:, 0:32], ones_row[:], brow_t[:], start=True, stop=True),
            wait=[e_brow], inc="cT")
        ev["bias_c"] = prog.op("scalar", lambda e: e.copy(bias_c[:], pqsq[:, 0:32]),
                               wait=[e_bcm], inc="cS")

        A3 = A_t[:].rearrange("p (a b) -> p a b", b=RANK)
        tr = None
        for ic in range(NI):
            tr = prog.op("tensor", lambda e, ic=ic: e.matmul(
                pAt[:, RANK * ic:RANK * (ic + 1)], lA_bf[:, 128 * ic:128 * (ic + 1)],
                ident_bf[0:16, 0:16], start=True, stop=True),
                wait=[e_lA] if ic == 0 else [],
                inc="cT" if ic == NI - 1 else None)
        e_Atev = prog.op("scalar", lambda e: e.copy(A_t[:, 0:NI * RANK], pAt[:, 0:NI * RANK]),
                         wait=[tr], inc="cS")
        e_udone = None
        for tc in range(2):
            mmu = None
            for ic in range(NI):
                w = [e_Atev] if ic == 0 and tc == 0 else []
                mmu = prog.op("tensor", lambda e, tc=tc, ic=ic, A3=A3: e.matmul(
                    pq_a[0:16, 512 * tc:512 * (tc + 1)], A3[:, ic, :],
                    xn_own[:, ic, 512 * tc:512 * (tc + 1)],
                    start=(ic == 0), stop=(ic == NI - 1)),
                    wait=w, inc="cT" if ic == NI - 1 else None)
            e_udone = prog.op("vector", lambda e, tc=tc: e.tensor_single_scalar(
                u_sc[:, 512 * tc:512 * (tc + 1)], pq_a[0:16, 512 * tc:512 * (tc + 1)],
                LORA_SC, ALU.mult), wait=[mmu], inc="cV")
        ev["u_done"] = e_udone
        if stop_after == "B":
            e_bo = prog.dma("sync", d_out[0:128, 0:2048],
                            ksq_scr[:, 0:4096].bitcast(F32), "dO0",
                            wait=[ev["u_done"]])
            prog.op("sync", None, wait=[e_bo])

    if stop_after == "B":
        return _finish(nc, es, prog, open_guards)
    g_xn.__exit__(None, None, None)
    open_guards.remove(g_xn)
    g_attn = nc.sbuf_tensor("attn_own", [128, NI, NOWN], BF16, side="right")
    attn_own = g_attn.__enter__()
    open_guards.append(g_attn)

    with nc.sbuf_tensor("probsT", [128, 3, 1024], BF16) as probsT, \
         nc.sbuf_tensor("zeros_bf", [128, 512], BF16) as zeros_bf, \
         nc.sbuf_tensor("recip_sb", [1, 512], F32) as recip_sb, \
         nc.sbuf_tensor("recip_bf", [1, 512], BF16) as recip_bf, \
         nc.sbuf_tensor("recip_bc", [128, 512], F32) as recip_bc, \
         nc.psum_tensor("pS_a", [128, 1024], F32) as pS_a, \
         nc.psum_tensor("pS_b", [128, 1024], F32) as pS_b, \
         nc.psum_tensor("pAV", [128, 512], F32) as pAV, \
         nc.psum_tensor("psums", [128, 512], F32) as psums, \
         nc.psum_tensor("pbc", [128, 512], F32) as pbc:

        pS = [pS_a, pS_b]
        e_ms0 = prog.op("vector", lambda e: e.memset(pS_a[:], 0.0),
                        wait=[ev["u_done"]], inc="cV")
        e_ms1 = prog.op("vector", lambda e: e.memset(pS_b[:], 0.0),
                        wait=[ev["u_done"]], inc="cV")
        e_zero = prog.op("vector", lambda e: e.memset(zeros_bf[:], 0.0),
                         wait=[ev["u_done"]], inc="cV")
        eS_free = {0: e_ms0, 1: e_ms1}
        eP_free = {0: None, 1: None, 2: None}
        e_avread = None
        e_sumread = None
        pslot_i = 0
        pend_epi = None

        def emit_epilogue(epi):
            nonlocal e_avread, e_sumread
            e_cl, h_e, b_e = epi
            e_rc = prog.op("vector", lambda e: e.reciprocal(
                recip_sb[:], psums[0:1, :]), wait=[e_cl], inc="cV")
            e_sumread = e_rc
            e_rcp = prog.op("scalar", lambda e: e.copy(recip_bf[:], recip_sb[:]),
                            wait=[e_rc], inc="cS")
            e_bcm2 = prog.op("tensor", lambda e: e.matmul(
                pbc[:], ones_row[:], recip_bf[:], start=True, stop=True),
                wait=[e_rcp], inc="cT")
            e_bce = prog.op("vector", lambda e: e.tensor_copy(
                recip_bc[:], pbc[:]), wait=[e_bcm2], inc="cV")
            e_avread = prog.op("vector", lambda e, h=h_e, beta=b_e: e.tensor_mul(
                attn_own[:, h, 512 * beta:512 * (beta + 1)], pAV[:], recip_bc[:]),
                wait=[e_bce], inc="cV")
        for h in range(H):
            for beta in range(2):
                nkc = 16 * (beta + 1)
                e_pair_last = None
                pend = None

                def emit_av(pair):
                    nonlocal e_pair_last
                    kc0p, ps_p, w_p, gate_p = pair
                    last = None
                    for kk in range(2):
                        kc = kc0p + kk
                        qoff = 512 - w_p
                        w = []
                        if kk == 0:
                            w = [gate_p]
                            if kc == 0:
                                w += [e_avread, e_sumread]
                        prog.op("tensor", lambda e, kc=kc, ps_i=ps_p, kk=kk,
                                w_kc=w_p, qoff=qoff: e.matmul(
                            pAV[:, qoff:512], V_tok[:, kc, :],
                            probsT[:, ps_i, 512 * kk:512 * kk + w_kc],
                            start=(kc == 0), stop=False), wait=w)
                        last = prog.op("tensor", lambda e, kc=kc, ps_i=ps_p, kk=kk,
                                       w_kc=w_p, qoff=qoff: e.matmul(
                            psums[0:1, qoff:512], ones_col[:],
                            probsT[:, ps_i, 512 * kk:512 * kk + w_kc],
                            start=(kc == 0), stop=False), inc="cT")
                    eP_free[ps_p] = last
                    e_pair_last = last

                for kc0 in range(0, nkc, 2):
                    slot = (kc0 // 2) % 2
                    w_kc = 512 - 128 * max(0, kc0 // 4 - 4 * beta)
                    off = 512 * beta + (512 - w_kc)
                    mmS = None
                    for kk in range(2):
                        w = []
                        if kk == 0:
                            w = [eS_free[slot]]
                            if h == 0 and beta == 0 and kc0 == 0:
                                w.append(ev["bias_c"])
                        kc = kc0 + kk
                        mmS = prog.op("tensor", lambda e, slot=slot, kk=kk, kc=kc,
                                      w_kc=w_kc, h=h, off=off: e.matmul(
                            pS[slot][:, 512 * kk:512 * kk + w_kc],
                            K_fm[:, 128 * kc:128 * (kc + 1)],
                            q_own[:, h, off:off + w_kc],
                            start=True, stop=True),
                            wait=w, inc="cT" if kk == 1 else None)
                    if kc0 == 0 and pend_epi is not None:
                        emit_epilogue(pend_epi)
                        pend_epi = None
                    ps_i = pslot_i % 3
                    pslot_i += 1
                    e_exp = prog.op("scalar", lambda e, slot=slot, ps_i=ps_i,
                                    w_kc=w_kc, h=h: e.activation(
                        probsT[:, ps_i, 0:512 + w_kc], pS[slot][:, 0:512 + w_kc],
                        AF.Exp, bias=bias_c[:, h:h + 1], scale=scale_col[:]),
                        wait=[mmS, eP_free[ps_i],
                              ev["scale_col"] if (h == 0 and beta == 0) else None],
                        inc="cS")
                    eS_free[slot] = e_exp
                    e_gate = e_exp
                    if kc0 // 4 >= 4 * beta:
                        for kk in range(2):
                            j = (kc0 + kk) % 4
                            e_gate = prog.op("vector", lambda e, ps_i=ps_i, kk=kk,
                                             w_kc=w_kc, j=j: e.tensor_mul(
                                probsT[:, ps_i, 512 * kk:512 * kk + 128],
                                probsT[:, ps_i, 512 * kk:512 * kk + 128],
                                masks[:, j, :]),
                                wait=[e_exp, ev["consts"]] if kk == 0 else [],
                                inc="cV" if kk == 1 else None)
                    if pend is not None:
                        emit_av(pend)
                    pend = (kc0, ps_i, w_kc, e_gate)
                emit_av(pend)
                prog.op("tensor", lambda e: e.matmul(
                    pAV[:], V_tok[:, 0, :], zeros_bf[:], start=False, stop=True),
                    wait=[e_zero])
                e_pair_last = prog.op("tensor", lambda e: e.matmul(
                    psums[0:1, :], ones_col[:], zeros_bf[:], start=False, stop=True),
                    inc="cT")
                pend_epi = (e_pair_last, h, beta)
        emit_epilogue(pend_epi)
        ev["attn_done"] = e_avread
        if stop_after == "C":
            e_bo = prog.dma("sync", d_out[0:128, 0:512], recip_bc[:, 0:512],
                            "dO0", wait=[ev["attn_done"]])
            prog.op("sync", None, wait=[e_bo])

    if stop_after == "C":
        return _finish(nc, es, prog, open_guards)

    for g in (g_q, g_V, g_K):
        g.__exit__(None, None, None)
        open_guards.remove(g)

    g_xn2 = nc.sbuf_tensor("xn_own2", [128, NI, NOWN], BF16, side="right")
    xn2 = g_xn2.__enter__()
    open_guards.append(g_xn2)

    with nc.sbuf_tensor("xstrip2", [128, 2, DM], F32) as xstrip2, \
         nc.sbuf_tensor("xn_nat2", [128, 2, DM], BF16) as xn_nat2, \
         nc.sbuf_tensor("b32_scr", [32, 128], BF16) as b32_scr, \
         nc.sbuf_tensor("lB_bf", [128, NI, RANK], BF16) as lB_bf, \
         nc.psum_tensor("ptr2_a", [128, 512], F32) as ptr2_a, \
         nc.psum_tensor("ptr2_b", [128, 512], F32) as ptr2_b, \
         nc.psum_tensor("pmisc", [128, 512], F32) as pmisc, \
         nc.psum_tensor("pmiscb", [128, 512], F32) as pmiscb:

        ptr2 = [ptr2_a, ptr2_b]
        eA_prev = ev["attn_done"]
        eT_prev2 = ev["a_prev"][2]
        evc = None
        nreb = 0 if stop_after == "D1b" else (2 if stop_after in ("D1c", "D1d") else NOSTRIP)
        skip_tr = stop_after == "D1d"
        for s in range(nreb):
            p = s % 2
            e_ld = prog.dma("sync", xstrip2[:, p, :], d_xo[128 * s:128 * (s + 1), :],
                            "dX0" if p == 0 else "dX1", wait=[eA_prev],
                            tag=f"Dxld{s}")
            e_apply = prog.op("vector", lambda e, p=p, s=s: e.tensor_scalar(
                xn_nat2[:, p, :], xstrip2[:, p, :], mean_own[:, s:s + 1],
                rstd_own[:, s:s + 1], ALU.subtract, ALU.mult),
                wait=[e_ld, eT_prev2], inc="cV")
            eA_prev = e_apply
            if skip_tr:
                evc = e_apply
                continue
            tr_last = None
            for j in range(8):
                pj = ptr2[j % 2]
                for u in range(4):
                    ic = 4 * j + u
                    w = []
                    if u == 0:
                        w = [e_apply, ev.get(("D", "ptr_free", j % 2))]
                    tr_last = prog.op("tensor", lambda e, p=p, ic=ic, u=u, pj=pj:
                                      e.matmul(pj[:, 128 * u:128 * (u + 1)],
                                               xn_nat2[:, p, 128 * ic:128 * (ic + 1)],
                                               ident_bf[:], start=True, stop=True),
                                      wait=w, inc="cT" if u == 3 else None)
                evc = prog.op("scalar", lambda e, j=j, pj=pj, s=s: e.copy(
                    xn2[:, 4 * j:4 * (j + 1), 128 * s:128 * s + 128],
                    pj[:].rearrange("p (a b) -> p a b", b=128)),
                    wait=[tr_last], inc="cS")
                ev[("D", "ptr_free", j % 2)] = evc
            eT_prev2 = tr_last
        ev["xn2_done"] = evc if evc is not None else ev["attn_done"]
        if stop_after in ("D1c", "D1d"):
            e_bo = prog.dma("sync", d_out[0:128, 0:1024],
                            xn_nat2[:, 0, :].bitcast(F32)[:, 0:1024], "dO0",
                            wait=[ev["xn2_done"]])
            prog.op("sync", None, wait=[e_bo])
        if stop_after == "D1a":
            e_bo = prog.dma("sync", d_out[0:128, 0:1024],
                            xn_nat2[:, 0, :].bitcast(F32)[:, 0:1024], "dO0",
                            wait=[ev["xn2_done"]])
            prog.op("sync", None, wait=[e_bo])
        if stop_after == "D1c":
            pass

        prog.dma("gpsimd", b32_scr[:],
                 d_bias.rearrange("(c p) -> c p", p=128), "dPG",
                 wait=[ev["attn_done"]])
        e_lB = prog.dma("gpsimd", lB_bf[:], d_lB.rearrange("(c p) r -> p c r", p=128),
                        "dPG")
        trb = prog.op("tensor", lambda e: e.matmul(
            pmisc[:, 0:32], b32_scr[:], ident_bf[0:32, 0:32],
            start=True, stop=True), wait=[e_lB], inc="cT")
        ev["bias_t"] = prog.op("scalar", lambda e: e.copy(bias_t[:], pmisc[:, 0:32]),
                               wait=[trb], inc="cS")
        e_Bt = {}
        for g in range(NI // 4):
            trc = None
            for u in range(4):
                cc = 4 * g + u
                w = []
                if u == 0:
                    w = [e_lB, e_Bt.get(g - 1)]
                trc = prog.op("tensor", lambda e, cc=cc, u=u: e.matmul(
                    pmiscb[0:16, 128 * u:128 * (u + 1)], lB_bf[:, cc, :],
                    ident_bf[:], start=True, stop=True),
                    wait=w, inc="cT" if u == 3 else None)
            e_Bt[g] = prog.op("scalar", lambda e, g=g: e.copy(
                B_t[:, 512 * g:512 * (g + 1)], pmiscb[0:16, 0:512]),
                wait=[trc], inc="cS")
        ev["B_t"] = e_Bt[NI // 4 - 1]
        if stop_after == "D1b":
            e_bo = prog.dma("sync", d_out[0:16, 0:512],
                            B_t[0:16, 0:1024].bitcast(F32),
                            "dO0", wait=[ev["B_t"], ev["bias_t"]])
            prog.op("sync", None, wait=[e_bo])
        if stop_after == "D1":
            e_bo = prog.dma("sync", d_out[0:128, 0:1024],
                            xn_nat2[:, 0, :].bitcast(F32)[:, 0:1024], "dO0",
                            wait=[ev["xn2_done"], ev["B_t"], ev["bias_t"]])
            prog.op("sync", None, wait=[e_bo])

    if stop_after in ("D1", "D1a", "D1b", "D1c", "D1d"):
        return _finish(nc, es, prog, open_guards)

    with nc.sbuf_tensor("qw_bf", [128, DM], BF16) as qw_bf, \
         nc.sbuf_tensor("qs_scr", [128, 1024], F32) as qs_scr, \
         nc.sbuf_tensor("wn_strip", [128, 2, 1024], BF16) as wn_strip, \
         nc.sbuf_tensor("Wt", [128, 2, NI, 128], BF16) as Wt, \
         nc.sbuf_tensor("wo_bf", [128, 2, NI, 128], BF16) as wo_bf, \
         nc.sbuf_tensor("sz_t", [128, 2, 2, GS], F32) as sz_t, \
         nc.sbuf_tensor("zs_t", [128, 2, GS], F32) as zs_t, \
         nc.sbuf_tensor("s_fm", [128, 1024], BF16) as s_fm, \
         nc.sbuf_tensor("ostage", [128, 8, 128], F32) as ostage, \
         nc.sbuf_tensor("xsl", [128, 8, 128], F32) as xsl, \
         nc.psum_tensor("pout_a", [128, 1024], F32) as pout_a, \
         nc.psum_tensor("pout_b", [128, 1024], F32) as pout_b, \
         nc.psum_tensor("pWt_a", [128, 512], F32) as pWt_a, \
         nc.psum_tensor("pWt_b", [128, 512], F32) as pWt_b, \
         nc.psum_tensor("pT_a", [128, 512], F32) as pT_a, \
         nc.psum_tensor("pT_b", [128, 512], F32) as pT_b:

        pout = [pout_a, pout_b]
        pWt = [pWt_a, pWt_b]
        pT = [pT_a, pT_b]
        wo_view = d_wo.rearrange("(c p) o -> p c o", p=128)
        e_qwld = {}
        e_szld = {}
        e_wold = {}
        e_Wt_done = {}
        e_mm_done = {}
        e_sfm = {}
        e_trlast = {}

        def prefetch(oc, qw_too=True):
            if oc >= NI:
                return
            par = oc % 2
            gate = ([ev["attn_done"], ev["xn2_done"], ev["B_t"], ev["bias_t"]]
                    if oc < 2 else [])
            sq = "dWQ0" if par == 0 else "dWQ1"
            if qw_too:
                prog.dma("gpsimd", qw_bf[:], d_qw[128 * oc:128 * (oc + 1), :],
                         sq, wait=gate + [e_Wt_done.get(oc - 1)], tag=f"qw{oc}")
            prog.dma("gpsimd", sz_t[:, par, 0, :], d_sc[128 * oc:128 * (oc + 1), :], sq)
            e_qwld[oc] = prog.dma("gpsimd", sz_t[:, par, 1, :],
                                  d_zr[128 * oc:128 * (oc + 1), :], sq)
            e_szld[oc] = e_qwld[oc]
            e_wold[oc] = prog.dma("gpsimd", wo_bf[:, par, :, :],
                                  wo_view[:, :, 128 * oc:128 * (oc + 1)],
                                  "dWO0" if par == 0 else "dWO1",
                                  wait=gate + [e_mm_done.get(oc - 2)], tag=f"wo{oc}")

        n_oc = 2 if (stop_after or "").startswith("D2") else NI
        d_lvl = {"D2s": 0, "D2t": 1, "D2x": 2, "D2o": 3, "D2": 9}.get(stop_after, 9)
        prefetch(0)
        prefetch(1, qw_too=False)
        e_qw1 = {}
        assert True
        for oc in range(n_oc):
            par = oc % 2
            e_zs = prog.op("vector", lambda e, par=par: e.tensor_mul(
                zs_t[:, par, :], sz_t[:, par, 1, :], sz_t[:, par, 0, :]),
                wait=[e_szld[oc], e_qwld[oc], e_qw1.get(oc),
                      e_Wt_done.get(oc - 2)], inc="cV")
            e_Wev = None
            e_wn_prev = None
            for st in range(4):
                sp = st % 2
                s_bc = sz_t[:, par, 0, 16 * st:16 * (st + 1)].unsqueeze(2) \
                    .broadcast_to([128, 16, GS])
                zs_bc = zs_t[:, par, 16 * st:16 * (st + 1)].unsqueeze(2) \
                    .broadcast_to([128, 16, GS])
                q3 = qw_bf[:, 1024 * st:1024 * (st + 1)] \
                    .rearrange("p (g r) -> p g r", r=GS)
                w = [e_zs] if st == 0 else []
                w.append(ev.get(("wn_free", sp)))
                w.append(e_wn_prev)
                e_tt1 = prog.op("vector", lambda e, q3=q3, s_bc=s_bc: e.tensor_tensor(
                    qs_scr[:].rearrange("p (g r) -> p g r", r=GS), q3, s_bc, ALU.mult),
                    wait=w, inc="cV")
                e_wn = prog.op("vector", lambda e, sp=sp, zs_bc=zs_bc: e.tensor_tensor(
                    wn_strip[:, sp, :].rearrange("p (g r) -> p g r", r=GS),
                    qs_scr[:].rearrange("p (g r) -> p g r", r=GS), zs_bc, ALU.subtract),
                    wait=[e_tt1, e_zs], inc="cV")
                e_wn_prev = e_wn
                e_Wev = None
                for h2 in range(2):
                    pws = pWt[h2]
                    trl = None
                    for u in range(4):
                        w2 = [e_wn, ev.get(("pWt_free", h2))] if u == 0 else []
                        trl = prog.op("tensor", lambda e, sp=sp, u=u, h2=h2, pws=pws:
                                      e.matmul(pws[:, 128 * u:128 * (u + 1)],
                                               wn_strip[:, sp, 128 * (4 * h2 + u):
                                                        128 * (4 * h2 + u + 1)],
                                               ident_bf[:], start=True, stop=True),
                                      wait=w2, inc="cT" if u == 3 else None)
                    e_Wev = prog.op("scalar", lambda e, par=par, st=st, h2=h2, pws=pws:
                                    e.copy(Wt[:, par, 8 * st + 4 * h2:8 * st + 4 * h2 + 4, :],
                                           pws[:].rearrange("p (a b) -> p a b", b=128)),
                                    wait=[trl], inc="cS")
                    ev[("pWt_free", h2)] = e_Wev
                ev[("wn_free", sp)] = e_Wev
            e_Wt_done[oc] = e_Wev
            if oc + 1 < n_oc:
                sq1 = "dWQ0" if (oc + 1) % 2 == 0 else "dWQ1"
                e_qw1[oc + 1] = prog.dma(
                    "gpsimd", qw_bf[:], d_qw[128 * (oc + 1):128 * (oc + 2), :],
                    sq1, wait=[e_Wt_done[oc]], tag=f"qw{oc + 1}")
            mm = None
            for tc in range(2):
                for ic in range(NI):
                    w = []
                    if tc == 0 and ic == 0:
                        w = [e_Wt_done[oc], e_wold[oc], ev["attn_done"], ev["B_t"],
                             ev["u_done"], ev["xn2_done"], e_sfm.get(oc - 2)]
                    prog.op("tensor", lambda e, par=par, tc=tc, ic=ic: e.matmul(
                        pout[par][:, 512 * tc:512 * (tc + 1)],
                        Wt[:, par, ic, :].squeeze(1) if False else Wt[:, par, ic, :],
                        xn2[:, ic, 512 * tc:512 * (tc + 1)],
                        start=(ic == 0), stop=False), wait=w)
                    prog.op("tensor", lambda e, par=par, tc=tc, ic=ic: e.matmul(
                        pout[par][:, 512 * tc:512 * (tc + 1)], wo_bf[:, par, ic, :],
                        attn_own[:, ic, 512 * tc:512 * (tc + 1)],
                        start=False, stop=False))
                mm = prog.op("tensor", lambda e, par=par, tc=tc, oc=oc: e.matmul(
                    pout[par][:, 512 * tc:512 * (tc + 1)],
                    B_t[:, 128 * oc:128 * (oc + 1)],
                    u_sc[:, 512 * tc:512 * (tc + 1)], start=False, stop=True),
                    inc="cT" if tc == 1 else None)
            e_mm_done[oc] = mm
            prefetch(oc + 2, qw_too=False)
            e_sfm[oc] = prog.op("vector", lambda e, par=par, oc=oc: e.tensor_single_scalar(
                s_fm[:], pout[par][:], bias_t[:, oc:oc + 1], ALU.add),
                wait=[mm, ev["bias_t"], e_trlast.get(oc - 1)], inc="cV")
            if d_lvl == 0:
                e_trlast[oc] = e_sfm[oc]
                continue
            e_x = prog.dma("sync", xsl[:],
                           d_xo.rearrange("(j p) o -> p j o", p=128)
                           [:, :, 128 * oc:128 * (oc + 1)],
                           "dX0", wait=[ev.get("xsl_read")], tag=f"xsl{oc}")
            e_add = None
            for j in range(8):
                bk = (j // 4) % 2
                pts = pT[bk]
                w = [e_sfm[oc]] if j % 4 == 0 else []
                if j % 4 == 0:
                    w.append(ev.get(("pT_free", bk)))
                trj = prog.op("tensor", lambda e, j=j, pts=pts: e.matmul(
                    pts[:, 128 * (j % 4):128 * (j % 4) + 128],
                    s_fm[:, 128 * j:128 * (j + 1)], ident_bf[:],
                    start=True, stop=True), wait=w, inc="cT")
                if j == 7:
                    e_trlast[oc] = trj
                if d_lvl == 1:
                    ev[("pT_free", bk)] = trj
                    continue
                e_add = prog.op("vector", lambda e, j=j, pts=pts: e.tensor_add(
                    ostage[:, j, :], pts[:, 128 * (j % 4):128 * (j % 4) + 128],
                    xsl[:, j, :]),
                    wait=[trj, e_x, ev.get("ostage_free")], inc="cV")
                if j % 4 == 3:
                    ev[("pT_free", bk)] = e_add
            if d_lvl >= 2:
                ev["xsl_read"] = e_add
            if d_lvl in (2, 3):
                continue
            e_od = prog.dma("gpsimd",
                            d_out.rearrange("(j p) o -> p j o", p=128)
                            [:, :, 128 * oc:128 * (oc + 1)],
                            ostage[:], "dO0", wait=[e_add], tag=f"out{oc}")
            ev["ostage_free"] = e_od
        if d_lvl >= 9 or prog.cnt["dO0"] + prog.cnt["dO1"] > 0:
            prog.op("sync", None, wait=[("dO0", prog.cnt["dO0"]),
                                        ("dO1", prog.cnt["dO1"])])
            prog.op("gpsimd", None, wait=[("dO0", prog.cnt["dO0"]),
                                          ("dO1", prog.cnt["dO1"])])
        else:
            last = e_trlast.get(n_oc - 1) if d_lvl == 1 else None
            ws = [w for w in (e_sfm.get(n_oc - 1), last,
                              ev.get(("pT_free", 0)) if d_lvl == 2 else None) if w]
            e_bo = prog.dma("sync", d_out[0:128, 0:512],
                            recip_bc[:, 0:512] if False else s_fm[:, 0:1024].bitcast(F32)[:, 0:512],
                            "dO0", wait=ws)
            prog.op("sync", None, wait=[e_bo])

    for g in (g_xn2, g_attn):
        g.__exit__(None, None, None)
        open_guards.remove(g)

    return _finish(nc, es, prog, open_guards)


def _finish(nc, es, prog, open_guards=()):
    for g in reversed(list(open_guards)):
        g.__exit__(None, None, None)
    sems = {s: es.enter_context(nc.semaphore(s)) for s in SEMS}
    block = es.enter_context(nc.Block())
    watermark = {e: {s: 0 for s in SEMS} for e in ENGS}

    def emit_engine(eng_name):
        oplist = prog.ops[eng_name]
        wm = watermark[eng_name]

        def body(eng):
            for fn, waits, inc, tag in oplist:
                merged = {}
                for (sem, val) in waits:
                    merged[sem] = max(merged.get(sem, 0), val)
                for sem, val in merged.items():
                    if val > wm[sem]:
                        eng.wait_ge(sems[sem], val)
                        wm[sem] = val
                instr = fn(eng) if fn is not None else None
                if inc is not None:
                    sem, n = inc
                    instr.then_inc(sems[sem], n)
        return body

    for e in ENGS:
        getattr(block, e)(emit_engine(e))

    es.close()
    return nc


_NC_CACHE = None


def _get_nc():
    global _NC_CACHE
    if _NC_CACHE is None:
        _NC_CACHE = build_nc()
    return _NC_CACHE


def kernel(L0_attn_x, L0_attn_wq, L0_attn_wk, L0_attn_wv, L0_attn_wo,
           L0_ffn_qweight, L0_ffn_scales, L0_ffn_zeros, L0_ffn_lora_A,
           L0_ffn_lora_B, L0_ffn_bias):
    import ml_dtypes
    x = np.ascontiguousarray(np.asarray(L0_attn_x, dtype=np.float32))
    wq = np.ascontiguousarray(np.asarray(L0_attn_wq, dtype=np.float32))
    wk = np.ascontiguousarray(np.asarray(L0_attn_wk, dtype=np.float32))
    wv = np.ascontiguousarray(np.asarray(L0_attn_wv, dtype=np.float32))
    wo = np.ascontiguousarray(np.asarray(L0_attn_wo, dtype=np.float32))
    qw = np.ascontiguousarray(np.asarray(L0_ffn_qweight, dtype=np.int32))
    sc = np.ascontiguousarray(np.asarray(L0_ffn_scales, dtype=np.float32))
    zr = np.ascontiguousarray(np.asarray(L0_ffn_zeros, dtype=np.float32))
    lA = np.ascontiguousarray(np.asarray(L0_ffn_lora_A, dtype=np.float32))
    lB = np.ascontiguousarray(np.asarray(L0_ffn_lora_B, dtype=np.float32))
    bias = np.ascontiguousarray(np.asarray(L0_ffn_bias, dtype=np.float32))

    ident_bf = np.eye(128, dtype=ml_dtypes.bfloat16)
    ident_f = np.eye(128, dtype=np.float32)

    nc = _get_nc()
    in_maps = []
    for c in range(8):
        b, a = c // 4, c % 4
        p = np.arange(128)[None, :, None]
        f = np.arange(128)[None, None, :]
        jj = np.arange(4)[:, None, None]
        m01 = (4 * f + a >= 128 * jj + p).astype(ml_dtypes.bfloat16)
        in_maps.append({
            "x_b": x[b],
            "x_own": np.ascontiguousarray(x[b, a::4, :]),
            "wq": wq, "wk": wk, "wv": wv, "wo": wo,
            "qweight": qw, "scales": sc, "zeros": zr,
            "lora_A": lA, "lora_B": lB, "bias": bias,
            "ident_bf": ident_bf, "ident_f": ident_f,
            "masks01": np.ascontiguousarray(m01),
        })
    res = run_bass_kernel_spmd(nc, in_maps, core_ids=list(range(8)))
    out = np.empty((2, T, DM), dtype=np.float32)
    for c in range(8):
        b, a = c // 4, c % 4
        out[b, a::4, :] = res.results[c]["out_own"]
    return out
